# version 14
# baseline (speedup 1.0000x reference)
import numpy as np
import ml_dtypes
from contextlib import ExitStack

import concourse.bass as bass
import concourse.mybir as mybir
from concourse.bass_utils import run_bass_kernel_spmd

F32 = mybir.dt.float32
BF16 = mybir.dt.bfloat16
AF = mybir.ActivationFunctionType
ALU = mybir.AluOpType
AX = mybir.AxisListType

NCORE = 8
S = 4096
D = 1024
NCH = 8
NH = 16
NQ = 8
NI = 22
NM = 9
NG5 = 8
PHASE_LIMIT = 3
P1_LEVEL = 4

def kbs_of(Q):
    return 4 * (Q + 1)

TPH = sum(kbs_of(Q) for Q in range(NQ))
QB = [0]
for _q in range(NQ):
    QB.append(QB[-1] + kbs_of(_q))

def gti(h, Q, kb):
    return h * TPH + QB[Q] + kb

NTILE = NH * TPH

C_ID = 0
C_TRI = 128
C_ONESCOL = 256
C_ONESROW = 257
C_EPS = 385
C_NEGHALF = 386
C_HALF = 387
C_SEL = 392
C_MASK = C_SEL + 1024
C_W = C_MASK + 512


def build_nc():
    nc = bass.Bass("TRN2", target_bir_lowering=False)

    x_in = nc.dram_tensor("x_img", [128, NCH, S], F32, kind="ExternalInput").ap()
    y_out = nc.dram_tensor("y_img", [128, NCH, S], F32, kind="ExternalOutput").ap()
    wgu_in = nc.dram_tensor("wgu_img", [NM, 128, NI, 2048], F32, kind="ExternalInput").ap()
    wd_in = nc.dram_tensor("wd_img", [NM, 128, 8, 2816], F32, kind="ExternalInput").ap()
    gate_in = nc.dram_tensor("gate_img", [128, 64], F32, kind="ExternalInput").ap()
    const_in = nc.dram_tensor("const_img", [128, C_W], F32, kind="ExternalInput").ap()
    wgu_bf = nc.dram_tensor("wgu_bf", [NM, 128, NI, 2048], BF16, kind="Internal").ap()
    wd_bf = nc.dram_tensor("wd_bf", [NM, 128, 8, 2816], BF16, kind="Internal").ap()

    es = ExitStack()
    sb = es.enter_context

    kf = sb(nc.sbuf_tensor("kf", [128, 392], F32))
    idb = sb(nc.sbuf_tensor("idb", [128, 128], BF16))
    maskb = sb(nc.sbuf_tensor("maskb", [128, 512], BF16))
    onescol_b = sb(nc.sbuf_tensor("onescol_b", [128, 1], BF16))
    selb = sb(nc.sbuf_tensor("selb", [8, 1024], BF16))
    gimgb = sb(nc.sbuf_tensor("gimgb", [128, 64], BF16))
    x_sb = sb(nc.sbuf_tensor("x_sb", [128, NCH, S], F32))
    bounce = sb(nc.sbuf_tensor("bounce", [128, 3, 512], BF16))
    xb = sb(nc.sbuf_tensor("xb", [128, 2, 512], BF16))
    xsq = sb(nc.sbuf_tensor("xsq", [128, 2, 512], BF16))
    RA = sb(nc.sbuf_tensor("RA", [33, 512], F32))
    RB = sb(nc.sbuf_tensor("RB", [33, 512], F32))
    r0 = RA[0:1, :]
    r1 = RA[32:33, :]
    r3v = RB[0:1, :]
    r2 = RB[32:33, :]
    r4 = sb(nc.sbuf_tensor("r4", [1, 512], F32))
    mr_col = sb(nc.sbuf_tensor("mr_col", [128, 32], F32))
    r_col = sb(nc.sbuf_tensor("r_col", [128, 32], F32))

    id_f = kf[:, C_ID:C_ID + 128]
    id1 = kf[0:1, 0:1]
    ones1x128f = kf[0:1, C_ONESROW:C_ONESROW + 128]
    ones1x64f = kf[0:1, C_ONESROW:C_ONESROW + 64]
    eps_cell = kf[0:1, C_EPS:C_EPS + 1]
    neghalf_cell = kf[0:1, C_NEGHALF:C_NEGHALF + 1]
    half_cell = kf[0:1, C_HALF:C_HALF + 1]

    esB = ExitStack()
    h_tm = [esB.enter_context(nc.sbuf_tensor(f"h_tm{i}", [128, 32, 130], BF16)) for i in range(2)]
    h65 = [esB.enter_context(nc.sbuf_tensor(f"h65_{i}", [65, S], BF16)) for i in range(2)]
    aug = [esB.enter_context(nc.sbuf_tensor(f"aug{i}", [65, 512], BF16)) for i in range(4)]
    n2row = [esB.enter_context(nc.sbuf_tensor(f"n2row{i}", [1, 512], BF16)) for i in range(2)]
    sqt = [esB.enter_context(nc.sbuf_tensor(f"sqt{i}", [64, 512], BF16)) for i in range(2)]
    pt4 = esB.enter_context(nc.sbuf_tensor("pt4", [128, 2048], BF16))
    o_sb = [esB.enter_context(nc.sbuf_tensor(f"o_sb{i}", [65, 512], F32)) for i in range(2)]
    t1t = esB.enter_context(nc.sbuf_tensor("t1t", [128, 512], F32))
    rlr = [esB.enter_context(nc.sbuf_tensor(f"rlr{i}", [1, 512], F32)) for i in range(2)]
    cg = esB.enter_context(nc.sbuf_tensor("cg", [1, 8], F32))
    esB.close()

    esC = ExitStack()
    a_t = esC.enter_context(nc.sbuf_tensor("a_t", [128, NI, 512], BF16))
    h2s = esC.enter_context(nc.sbuf_tensor("h2s", [128, NCH, 512], BF16))
    wgu_buf = esC.enter_context(nc.sbuf_tensor("wgu_buf", [128, 3, 2048], BF16))
    wd_buf = esC.enter_context(nc.sbuf_tensor("wd_buf", [128, 3, 1408], BF16))
    sg = esC.enter_context(nc.sbuf_tensor("sg", [128, 2, 512], F32))
    cwt = esC.enter_context(nc.sbuf_tensor("cwt", [8, 512], BF16))
    cbc = esC.enter_context(nc.sbuf_tensor("cbc", [128, 512], F32))
    mean_b = esC.enter_context(nc.sbuf_tensor("mean_b", [128, 512], F32))
    rstd_b = esC.enter_context(nc.sbuf_tensor("rstd_b", [128, 512], F32))
    lg_sb = esC.enter_context(nc.sbuf_tensor("lg_sb", [128, 8], F32))
    e_g = esC.enter_context(nc.sbuf_tensor("e_g", [128, 8], F32))
    k_g = esC.enter_context(nc.sbuf_tensor("k_g", [128, 8], F32))
    cw_g = esC.enter_context(nc.sbuf_tensor("cw_g", [128, 8], F32))
    gcl = esC.enter_context(nc.sbuf_tensor("gcl", [128, 8], F32))
    esC.close()

    esP1 = ExitStack()
    ps_m1 = esP1.enter_context(nc.psum_tensor("ps_m1", [1, 512], F32))
    ps_m2 = esP1.enter_context(nc.psum_tensor("ps_m2", [1, 512], F32))
    ps_tc = esP1.enter_context(nc.psum_tensor("ps_tc", [128, 8], F32))
    esP1.close()
    esP2 = ExitStack()
    ps_s4 = esP2.enter_context(nc.psum_tensor("ps_s4", [128, 2048], F32))
    ps_o = esP2.enter_context(nc.psum_tensor("ps_o", [65, 512], F32))
    ps_bx = esP2.enter_context(nc.psum_tensor("ps_bx", [128, 512], F32))
    ps_bh = esP2.enter_context(nc.psum_tensor("ps_bh", [64, 512], BF16))
    ps_nrl = esP2.enter_context(nc.psum_tensor("ps_nrl", [65, 512], F32))
    esP2.close()
    esP3 = ExitStack()
    ps_gu = [esP3.enter_context(nc.psum_tensor(f"ps_gu{i}", [128, 512], F32)) for i in range(4)]
    ps_y = [esP3.enter_context(nc.psum_tensor(f"ps_y{i}", [128, 512], F32)) for i in range(2)]
    ps_ms = [esP3.enter_context(nc.psum_tensor(f"ps_ms{i}", [128, 512], F32)) for i in range(2)]
    esP3.close()

    sem_names = [
        "s_x", "s_k", "s_kb", "s_ld0", "s_st0",
        "s_wg0", "s_wg1", "s_wg2",
        "s_wd0", "s_wd1", "s_wd2", "s_out",
        "a1", "v1", "pxs", "a2", "v2", "p1t", "v3",
        "a_sq", "p_n2", "v_aug", "p_s", "a_exp", "p_pv",
        "v_rl", "p_rlb", "v_xup", "p_bx", "v_bx", "p_bh", "v_bh",
        "a13", "v13", "pxs3", "a23", "v23", "v_st", "p_ms", "v_gt1",
        "a_g", "v_gt2", "v_cw", "p_gu", "a_sil", "v_a", "p_y", "v_y",
        "v_cb", "v_h2",
    ]
    SM = {n: sb(nc.semaphore(n)) for n in sem_names}

    def mean_cnt(ci):
        return 2 * ci + 1

    def sq_cnt(ci):
        return 2 * ci + 2

    def head_builds(h):
        out = []
        if h >= 15:
            return out
        if h % 2 == 1:
            p = (h + 1) // 2
            out += [("bx", p, g) for g in range(8)]
        out += [("bh", h + 1, g) for g in range(8)]
        return out

    G5SL = [slice(g5 * 512, (g5 + 1) * 512) for g5 in range(NG5)]

    PC_PIECES = []
    PC_CUM = []
    _cum = 0
    for _m in range(NM):
        for _i in range(NI):
            for _qq in range(4):
                PC_PIECES.append(("wgu", _m, _i, 0, _qq * 512, 512))
            _cum += 4
            PC_CUM.append(_cum)
        for _hc in range(8):
            for _half in range(2):
                for _off, _w in ((0, 512), (512, 512), (1024, 384)):
                    PC_PIECES.append(("wd", _m, _hc, _half, _off, _w))
                _cum += 3
                PC_CUM.append(_cum)

    def _pc_aps(piece):
        kind, m_, a, b, off, w = piece
        if kind == "wgu":
            return (wgu_in[m_, :, a, off:off + w],
                    wgu_bf[m_, :, a, off:off + w], w)
        cs = b * 1408 + off
        return (wd_in[m_, :, a, cs:cs + w],
                wd_bf[m_, :, a, cs:cs + w], w)

    def emit_precast_loads(eng):
        for p, piece in enumerate(PC_PIECES):
            src_ap, _, w = _pc_aps(piece)
            if p >= 3:
                eng.wait_ge(SM["s_st0"], 16 * (p - 2))
            if PHASE_LIMIT >= 2 and 2 * p - 300 > 0:
                eng.wait_ge(SM["a_exp"], min(2 * p - 300, NTILE))
            eng.dma_start(out=bounce[:, p % 3, 0:w],
                          in_=src_ap).then_inc(SM["s_ld0"], 16)

    def emit_precast_stores(eng):
        for p, piece in enumerate(PC_PIECES):
            _, dst_ap, w = _pc_aps(piece)
            eng.wait_ge(SM["s_ld0"], 16 * (p + 1))
            eng.dma_start(out=dst_ap,
                          in_=bounce[:, p % 3, 0:w]).then_inc(SM["s_st0"], 16)

    with nc.Block() as block:

        @block.tensor
        def _(t):
            if PHASE_LIMIT < 1 or P1_LEVEL < 2:
                if PHASE_LIMIT >= 1 and P1_LEVEL >= 2:
                    pass
                if PHASE_LIMIT < 1 or P1_LEVEL < 2:
                    return
            for tt in range(8):
                for c in range(8):
                    ci = 8 * tt + c
                    t.wait_ge(SM["a1"], ci + 1)
                    if ci == 0:
                        t.wait_ge(SM["s_kb"], 16 * 5)
                        t.wait_ge(SM["s_k"], 16)
                    if tt > 0 and c == 0 and P1_LEVEL >= 3:
                        t.wait_ge(SM["a2"], 2 * (tt - 1) + 1)
                    t.matmul(ps_m1[:], onescol_b[:], xb[:, c % 2, :],
                             start=(c == 0), stop=(c == 7),
                             skip_group_check=True).then_inc(SM["pxs"], 1)
                    t.wait_ge(SM["v1"], ci + 1)
                    t.matmul(ps_m2[:], onescol_b[:], xsq[:, c % 2, :],
                             start=(c == 0), stop=(c == 7),
                             skip_group_check=True).then_inc(SM["pxs"], 1)
                if P1_LEVEL < 4:
                    continue
                t.wait_ge(SM["v2"], 2 * tt + 2)
                if tt > 0:
                    t.wait_ge(SM["v3"], tt)
                for j in range(4):
                    t.transpose(ps_tc[:, j:j + 1], r4[0:1, j * 128:(j + 1) * 128], id1)
                for j in range(4):
                    mmtc = t.transpose(ps_tc[:, 4 + j:5 + j], r3v[0:1, j * 128:(j + 1) * 128], id1)
                mmtc.then_inc(SM["p1t"], 1)

            if PHASE_LIMIT < 2:
                return
            def pe_build(kind, pidx, g):
                if kind == "bx":
                    if 8 * pidx + g >= 1:
                        t.wait_ge(SM["v_bx"], 8 * pidx + g)
                    for j in range(4):
                        tc = g * 4 + j
                        mm = t.transpose(ps_bx[:, j * 128:(j + 1) * 128],
                                         x_sb[:, pidx, tc * 128:(tc + 1) * 128], id_f)
                    mm.then_inc(SM["p_bx"], 1)
                else:
                    hb = pidx
                    t.wait_ge(SM["v_bh"], 8 * hb + g)
                    t.wait_ge(SM["v_bx"], 8 * (hb // 2) + g + 1)
                    for j in range(4):
                        tc = g * 4 + j
                        mm = t.transpose(ps_bh[0:64, j * 128:(j + 1) * 128],
                                         h_tm[(hb // 2) % 2][:, tc, 65 * (hb % 2):65 * (hb % 2) + 64],
                                         idb[:])
                    mm.then_inc(SM["p_bh"], 1)

            for g in range(8):
                pe_build("bx", 0, g)
            for g in range(8):
                pe_build("bh", 0, g)

            for h in range(NH):
                pair = h // 2
                builds = head_builds(h)
                bpos = 0
                def pe_n2(Qn):
                    AQn = 8 * h + Qn
                    t.wait_ge(SM["a_sq"], AQn + 1)
                    if AQn >= 1:
                        t.wait_ge(SM["v_aug"], AQn)
                    t.matmul(ps_nrl[64:65, :], onescol_b[0:64, :], sqt[Qn % 2][:],
                             start=True, stop=True,
                             skip_group_check=True).then_inc(SM["p_n2"], 1)

                def emit_pv(Q, kb):
                    T = gti(h, Q, kb)
                    AQ = 8 * h + Q
                    nkb = kbs_of(Q)
                    sk = max(128 * kb - 512 * Q, 0)
                    t.wait_ge(SM["a_exp"], T + 1)
                    if kb == 0:
                        if AQ >= 1:
                            t.wait_ge(SM["v_rl"], AQ)
                        if Q == 0:
                            t.wait_ge(SM["v_bx"], 8 * (pair + 1))
                    t.matmul(ps_o[0:65, sk:512],
                             h_tm[pair % 2][:, kb, 65 * (h % 2):65 * (h % 2) + 65],
                             pt4[:, (T % 4) * 512 + sk:(T % 4) * 512 + 512],
                             start=(kb == 0), stop=(kb == nkb - 1),
                             skip_group_check=True).then_inc(SM["p_pv"], 1)

                pe_n2(0)
                pe_n2(1)
                pend = []
                defer = []
                for Q in range(NQ):
                    AQ = 8 * h + Q
                    nkb = kbs_of(Q)
                    rlb_slot = 3 if Q == 0 else 5
                    for kb in range(nkb):
                        T = gti(h, Q, kb)
                        delta = 128 * kb - 512 * Q
                        sk = max(delta, 0)

                        if kb == rlb_slot and AQ >= 1:
                            pv = AQ - 1
                            t.wait_ge(SM["v_rl"], pv + 1)
                            if pv >= 1:
                                t.wait_ge(SM["v_xup"], pv)
                            t.matmul(ps_nrl[0:64, :], ones1x64f, rlr[pv % 2][:],
                                     start=True, stop=True,
                                     skip_group_check=True).then_inc(SM["p_rlb"], 1)
                        if kb == 0 and Q == 0:
                            t.wait_ge(SM["v_bh"], 8 * (h + 1))
                        if kb == 0:
                            t.wait_ge(SM["v_aug"], AQ + 1)
                        if T >= 4:
                            t.wait_ge(SM["a_exp"], T - 3)
                        diag = delta >= 0
                        sb0 = (T % 4) * 512
                        mm = t.matmul(ps_s4[:, sb0 + sk:sb0 + 512],
                                      h65[h % 2][0:65, kb * 128:(kb + 1) * 128],
                                      aug[Q % 4][0:65, sk:512],
                                      start=True, stop=(not diag),
                                      skip_group_check=True)
                        if diag:
                            mm = t.matmul(ps_s4[:, sb0 + sk:sb0 + 512],
                                          idb[:], maskb[:, 0:512 - sk],
                                          start=False, stop=True,
                                          skip_group_check=True)
                        mm.then_inc(SM["p_s"], 1)
                        pend.append((Q, kb))
                        if len(pend) > 2:
                            emit_pv(*pend.pop(0))
                        if kb == 1:
                            for job in defer:
                                job()
                            defer = []
                    if Q + 2 <= 7:
                        defer.append(lambda Qn=Q + 2: pe_n2(Qn))
                    for _ in range(2 if len(builds) > 8 else 1):
                        if bpos < len(builds):
                            b = builds[bpos]; bpos += 1
                            defer.append(lambda b=b: pe_build(*b))
                while pend:
                    emit_pv(*pend.pop(0))
                for job in defer:
                    job()
                defer = []
                while bpos < len(builds):
                    pe_build(*builds[bpos]); bpos += 1
            t.wait_ge(SM["v_rl"], 128)
            t.wait_ge(SM["v_xup"], 127)
            t.matmul(ps_nrl[0:64, :], ones1x64f, rlr[127 % 2][:],
                     start=True, stop=True,
                     skip_group_check=True).then_inc(SM["p_rlb"], 1)

            if PHASE_LIMIT < 3:
                return
            for g5 in range(NG5):
                base = 18 * g5
                for c in range(8):
                    ci = 8 * g5 + c
                    t.wait_ge(SM["a13"], ci + 1)
                    if c == 0 and g5 >= 1:
                        t.wait_ge(SM["v_cb"], 8 * (g5 - 1) + 7)
                        t.wait_ge(SM["v_cb"], 8 * (g5 - 1) + 8)
                    t.matmul(ps_ms[0][0:1, :], onescol_b[:], xb[:, c % 2, :],
                             start=(c == 0), stop=(c == 7),
                             skip_group_check=True).then_inc(SM["pxs3"], 1)
                    t.wait_ge(SM["v13"], ci + 1)
                    t.matmul(ps_ms[1][0:1, :], onescol_b[:], xsq[:, c % 2, :],
                             start=(c == 0), stop=(c == 7),
                             skip_group_check=True).then_inc(SM["pxs3"], 1)
                t.wait_ge(SM["v23"], 2 * g5 + 2)
                t.wait_ge(SM["a23"], 2 * g5 + 1)
                t.matmul(ps_ms[0][:], ones1x128f, r0,
                         start=True, stop=True).then_inc(SM["p_ms"], 1)
                t.matmul(ps_ms[1][:], ones1x128f, r3v,
                         start=True, stop=True).then_inc(SM["p_ms"], 1)
                def pe_lg(t1):
                    bank = t1 % 2
                    if t1 == 0:
                        t.wait_ge(SM["v_st"], 2 * g5 + 1)
                        t.wait_ge(SM["v_h2"], g5 + 1)
                    elif t1 == 1:
                        t.wait_ge(SM["v_st"], 2 * g5 + 2)
                    else:
                        t.wait_ge(SM["v_cw"], 4 * g5 + t1 - 1)
                    for kc in range(8):
                        mm = t.matmul(ps_ms[bank][:, 0:8],
                                      h2s[:, kc, t1 * 128:(t1 + 1) * 128],
                                      gimgb[:, kc * 8:(kc + 1) * 8],
                                      start=(kc == 0), stop=(kc == 7))
                    mm.then_inc(SM["p_ms"], 1)

                def pe_cwt(t1):
                    bank = t1 % 2
                    t.wait_ge(SM["v_gt1"], 4 * g5 + t1 + 1)
                    t.wait_ge(SM["v_gt2"], 4 * g5 + t1 + 1)
                    t.transpose(ps_ms[bank][0:8, 0:128], cw_g[:, 0:8],
                                id_f).then_inc(SM["p_ms"], 1)

                pe_lg(0)
                pe_lg(1)
                pe_cwt(0)
                pe_lg(2)
                pe_cwt(1)
                pe_lg(3)
                pe_cwt(2)
                pe_cwt(3)
                def emit_gu(m_, i):
                    GP = (g5 * NM + m_) * NI + i
                    t.wait_ge(SM[f"s_wg{GP % 3}"], 16 * (GP // 3 + 1))
                    if GP >= 2:
                        t.wait_ge(SM["v_a"], GP - 1)
                    if i == 0:
                        t.wait_ge(SM["v_h2"], g5 + 1)
                    bg = (GP % 2) * 2
                    bu = bg + 1
                    for k in range(8):
                        t.matmul(ps_gu[bg][:], wgu_buf[:, GP % 3, k * 256:k * 256 + 128],
                                 h2s[:, k, :], start=(k == 0), stop=(k == 7))
                    for k in range(8):
                        mm = t.matmul(ps_gu[bu][:], wgu_buf[:, GP % 3, k * 256 + 128:k * 256 + 256],
                                      h2s[:, k, :], start=(k == 0), stop=(k == 7))
                    mm.then_inc(SM["p_gu"], 1)

                for m in range(NM):
                    GM = g5 * NM + m
                    if m >= 1:
                        bank = (m - 1) % 2
                        if m <= 2:
                            t.wait_ge(SM["v_cw"], 4 * g5 + 2 + m)
                        else:
                            t.wait_ge(SM["v_cb"], 8 * g5 + m - 2)
                        t.matmul(ps_ms[bank][:], selb[:, (m - 1) * 128:m * 128],
                                 cwt[0:8, :], start=True, stop=True).then_inc(SM["p_ms"], 1)
                    for i in (range(NI) if m == 0 else range(1, NI)):
                        emit_gu(m, i)
                    if m < NM - 1:
                        emit_gu(m + 1, 0)
                    for hc in range(8):
                        GY = GM * 8 + hc
                        if hc == 0:
                            t.wait_ge(SM["v_a"], GM * NI + NI)
                        if GY >= 2:
                            t.wait_ge(SM["v_y"], GY - 1)
                        for half in range(2):
                            WH = 2 * GY + half
                            t.wait_ge(SM[f"s_wd{WH % 3}"], 16 * (WH // 3 + 1))
                            for il in range(11):
                                ic = half * 11 + il
                                mm = t.matmul(ps_y[GY % 2][:],
                                              wd_buf[:, WH % 3, il * 128:(il + 1) * 128],
                                              a_t[:, ic, :],
                                              start=(ic == 0), stop=(ic == NI - 1))
                        mm.then_inc(SM["p_y"], 1)

        @block.scalar
        def _(s):
            if PHASE_LIMIT < 1:
                return
            for tt in range(8):
                for c in range(8):
                    ci = 8 * tt + c
                    if ci == 0:
                        s.wait_ge(SM["s_x"], 16)
                    if ci >= 2:
                        s.wait_ge(SM["v1"], ci - 1)
                        if P1_LEVEL >= 2:
                            s.wait_ge(SM["pxs"], mean_cnt(ci - 2))
                    s.copy(xb[:, c % 2, :],
                           x_sb[:, c, tt * 512:(tt + 1) * 512]).then_inc(SM["a1"], 1)
                if P1_LEVEL < 3:
                    continue
                s.wait_ge(SM["pxs"], 16 * tt + 16)
                if tt >= 1:
                    s.wait_ge(SM["v2"], 2 * (tt - 1) + 2)
                s.mul(r0, ps_m1[0:1, :], 1.0 / 1024.0)
                s.mul(r1, ps_m2[0:1, :], 1.0 / 1024.0)
                s.drain().then_inc(SM["a2"], 1)
                if ci == 7 and tt == 0:
                    s.wait_ge(SM["s_k"], 16)
                s.wait_ge(SM["v2"], 2 * tt + 1)
                if P1_LEVEL >= 4:
                    s.wait_ge(SM["p1t"], tt)
                s.activation(r2, r1, AF.Sqrt, bias=eps_cell)
                s.drain().then_inc(SM["a2"], 1)

            if PHASE_LIMIT < 2:
                return
            tile_sk = []
            for h in range(NH):
                for Q in range(NQ):
                    for kb in range(kbs_of(Q)):
                        tile_sk.append(max(128 * kb - 512 * Q, 0))
            T2 = 0
            while T2 < NTILE:
                sk0 = tile_sk[T2]
                if T2 % 2 == 0 and T2 + 1 < NTILE and tile_sk[T2 + 1] == 0:
                    b0 = (T2 % 4) * 512 + sk0
                    s.wait_ge(SM["p_s"], T2 + 2)
                    if T2 >= 4:
                        s.wait_ge(SM["p_pv"], T2 - 2)
                    s.activation(pt4[:, b0:b0 + 1024 - 512 + (512 - sk0)],
                                 ps_s4[:, b0:b0 + 1024 - 512 + (512 - sk0)],
                                 AF.Exp, scale=0.125).then_inc(SM["a_exp"], 2)
                    T2 += 2
                else:
                    b0 = (T2 % 4) * 512 + sk0
                    s.wait_ge(SM["p_s"], T2 + 1)
                    if T2 >= 5:
                        s.wait_ge(SM["p_pv"], T2 - 3)
                    s.activation(pt4[:, b0:b0 + 512 - sk0],
                                 ps_s4[:, b0:b0 + 512 - sk0],
                                 AF.Exp, scale=0.125).then_inc(SM["a_exp"], 1)
                    T2 += 1

            if PHASE_LIMIT < 3:
                return
            for g5 in range(NG5):
                base = 18 * g5
                GSL = G5SL[g5]
                for c in range(8):
                    ci = 8 * g5 + c
                    if ci == 0:
                        s.wait_ge(SM["v_xup"], 128)
                    if ci >= 2:
                        s.wait_ge(SM["v13"], ci - 1)
                        s.wait_ge(SM["pxs3"], mean_cnt(ci - 2))
                    s.copy(xb[:, c % 2, :], x_sb[:, c, GSL]).then_inc(SM["a13"], 1)
                s.wait_ge(SM["pxs3"], 16 * g5 + 16)
                if g5 >= 1:
                    s.wait_ge(SM["v23"], 2 * (g5 - 1) + 2)
                    s.wait_ge(SM["p_ms"], 18 * (g5 - 1) + 2)
                s.mul(r0, ps_ms[0][0:1, :], 1.0 / 1024.0)
                s.mul(r1, ps_ms[1][0:1, :], 1.0 / 1024.0)
                s.drain().then_inc(SM["a23"], 1)
                s.wait_ge(SM["v23"], 2 * g5 + 1)
                s.activation(r2, r1, AF.Sqrt, bias=eps_cell)
                s.drain().then_inc(SM["a23"], 1)
                for t1 in range(4):
                    s.wait_ge(SM["v_gt1"], 4 * g5 + t1 + 1)
                    s.activation(e_g[:], lg_sb[:], AF.Exp, bias=gcl[:, 0:1])
                    s.drain().then_inc(SM["a_g"], 1)
                for m in range(NM):
                    GM = g5 * NM + m
                    for i in range(NI):
                        GP = GM * NI + i
                        s.wait_ge(SM["p_gu"], GP + 1)
                        if GP >= 2:
                            s.wait_ge(SM["v_a"], GP - 1)
                        s.activation(sg[:, GP % 2, :], ps_gu[(GP % 2) * 2][:],
                                     AF.Silu).then_inc(SM["a_sil"], 1)

        @block.vector
        def _(v):
            if PHASE_LIMIT < 1:
                return
            for tt in range(8):
                for c in range(8):
                    ci = 8 * tt + c
                    v.wait_ge(SM["a1"], ci + 1)
                    if ci >= 2 and P1_LEVEL >= 2:
                        v.wait_ge(SM["pxs"], sq_cnt(ci - 2))
                    v.tensor_mul(xsq[:, c % 2, :], xb[:, c % 2, :],
                                 xb[:, c % 2, :]).then_inc(SM["v1"], 1)
                if P1_LEVEL < 3:
                    continue
                v.wait_ge(SM["a2"], 2 * tt + 1)
                v.tensor_mul(r2, r0, r0)
                v.drain()
                v.tensor_sub(r1, r1, r2)
                v.drain().then_inc(SM["v2"], 1)
                v.wait_ge(SM["a2"], 2 * tt + 2)
                if tt >= 1 and P1_LEVEL >= 4:
                    v.wait_ge(SM["p1t"], tt)
                v.reciprocal(r3v, r2)
                v.drain()
                v.tensor_mul(r4[0:1, :], r0, r3v)
                v.drain().then_inc(SM["v2"], 1)
                if P1_LEVEL < 4:
                    continue
                v.wait_ge(SM["p1t"], tt + 1)
                v.tensor_copy(mr_col[:, 4 * tt:4 * tt + 4], ps_tc[:, 0:4])
                v.tensor_copy(r_col[:, 4 * tt:4 * tt + 4], ps_tc[:, 4:8])
                v.drain().then_inc(SM["v3"], 1)

            if PHASE_LIMIT < 2:
                return
            v.wait_ge(SM["s_k"], 16)
            v.memset(h65[0][64:65, :], 1.0)
            v.memset(h65[1][64:65, :], 1.0)
            v.drain()

            def dve_build(kind, pidx, g):
                if kind == "bx":
                    v.wait_ge(SM["p_bx"], 8 * pidx + g + 1)
                    v.wait_ge(SM["v3"], g + 1)
                    if g == 0:
                        if pidx >= 2:
                            v.wait_ge(SM["p_pv"], TPH * (2 * pidx - 2))
                        v.memset(h_tm[pidx % 2][:, :, 64:65], 1.0)
                        v.memset(h_tm[pidx % 2][:, :, 129:130], 1.0)
                    for j in range(4):
                        tc = g * 4 + j
                        v.tensor_scalar(h_tm[pidx % 2][:, tc, 0:64],
                                        ps_bx[:, j * 128:j * 128 + 64],
                                        r_col[:, tc:tc + 1], mr_col[:, tc:tc + 1],
                                        ALU.mult, ALU.subtract)
                        v.tensor_scalar(h_tm[pidx % 2][:, tc, 65:129],
                                        ps_bx[:, j * 128 + 64:(j + 1) * 128],
                                        r_col[:, tc:tc + 1], mr_col[:, tc:tc + 1],
                                        ALU.mult, ALU.subtract)
                    v.drain().then_inc(SM["v_bx"], 1)
                else:
                    hb = pidx
                    v.wait_ge(SM["p_bh"], 8 * hb + g + 1)
                    if g == 0 and hb >= 2:
                        v.wait_ge(SM["p_s"], TPH * (hb - 1))
                        v.wait_ge(SM["a_sq"], 8 * (hb - 1))
                    for j in range(4):
                        tc = g * 4 + j
                        v.tensor_copy(h65[hb % 2][0:64, tc * 128:(tc + 1) * 128],
                                      ps_bh[0:64, j * 128:(j + 1) * 128])
                    v.drain().then_inc(SM["v_bh"], 1)

            def dve_sq(h, Qn):
                AQn = 8 * h + Qn
                if Qn == 0:
                    v.wait_ge(SM["v_bh"], 8 * (h + 1))
                if AQn >= 2:
                    v.wait_ge(SM["p_n2"], AQn - 1)
                v.tensor_mul(sqt[Qn % 2][:],
                             h65[h % 2][0:64, Qn * 512:(Qn + 1) * 512],
                             h65[h % 2][0:64, Qn * 512:(Qn + 1) * 512]
                             ).then_inc(SM["a_sq"], 1)

            def dve_pre(h, Q):
                AQ = 8 * h + Q
                v.wait_ge(SM["p_n2"], AQ + 1)
                if Q == 0:
                    v.wait_ge(SM["p_s"], TPH * h)
                v.tensor_copy(n2row[Q % 2][0:1, :], ps_nrl[64:65, :])
                v.drain()
                v.reduce_max(cg[:, 0:1], n2row[Q % 2][0:1, :], AX.X, negate=True)
                v.drain()
                if Q == 0:
                    v.tensor_copy(cg[:, 1 + (Q % 2):2 + (Q % 2)], cg[:, 0:1])
                else:
                    v.tensor_tensor(cg[:, 1 + (Q % 2):2 + (Q % 2)],
                                    cg[:, 1 + ((Q - 1) % 2):2 + ((Q - 1) % 2)],
                                    cg[:, 0:1], ALU.min)
                v.drain()
                v.tensor_scalar_mul(cg[:, 3:4], cg[:, 1 + (Q % 2):2 + (Q % 2)], half_cell)
                v.drain()
                v.tensor_scalar(aug[Q % 4][64:65, :], n2row[Q % 2][0:1, :],
                                neghalf_cell, cg[:, 3:4], ALU.mult, ALU.add)
                v.tensor_copy(aug[Q % 4][0:64, :], h65[h % 2][0:64, Q * 512:(Q + 1) * 512])
                v.drain().then_inc(SM["v_aug"], 1)

            def dve_finB(aq):
                hh, QQ = aq // 8, aq % 8
                rowbase = 64 * (hh % 2)
                c = hh // 2
                v.wait_ge(SM["p_rlb"], aq + 1)
                v.tensor_mul(t1t[rowbase:rowbase + 64, :], o_sb[aq % 2][0:64, :],
                             ps_nrl[0:64, :])
                v.drain()
                v.tensor_add(x_sb[rowbase:rowbase + 64, c, QQ * 512:(QQ + 1) * 512],
                             x_sb[rowbase:rowbase + 64, c, QQ * 512:(QQ + 1) * 512],
                             t1t[rowbase:rowbase + 64, :])
                v.drain().then_inc(SM["v_xup"], 1)

            for g in range(8):
                dve_build("bx", 0, g)
            for g in range(8):
                dve_build("bh", 0, g)

            for h in range(NH):
                builds = head_builds(h)
                bpos = 0
                dve_sq(h, 0)
                dve_pre(h, 0)
                dve_sq(h, 1)
                dve_pre(h, 1)
                for Q in range(NQ):
                    AQ = 8 * h + Q
                    v.wait_ge(SM["p_pv"], h * TPH + QB[Q + 1])
                    v.tensor_copy(o_sb[AQ % 2][:], ps_o[0:65, :])
                    v.drain()
                    v.reciprocal(rlr[AQ % 2][0:1, :], o_sb[AQ % 2][64:65, :])
                    v.drain().then_inc(SM["v_rl"], 1)
                    if AQ >= 1:
                        dve_finB(AQ - 1)
                    if Q + 2 <= 7:
                        dve_sq(h, Q + 2)
                    if Q + 2 <= 7:
                        dve_pre(h, Q + 2)
                    if bpos < len(builds):
                        dve_build(*builds[bpos]); bpos += 1
                    if bpos < len(builds) and len(builds) > 8:
                        dve_build(*builds[bpos]); bpos += 1
                while bpos < len(builds):
                    dve_build(*builds[bpos]); bpos += 1
            dve_finB(127)

            if PHASE_LIMIT < 3:
                return
            for g5 in range(NG5):
                base = 18 * g5
                GSL = G5SL[g5]
                for c in range(8):
                    ci = 8 * g5 + c
                    v.wait_ge(SM["a13"], ci + 1)
                    if ci >= 2:
                        v.wait_ge(SM["pxs3"], sq_cnt(ci - 2))
                    v.tensor_mul(xsq[:, c % 2, :], xb[:, c % 2, :],
                                 xb[:, c % 2, :]).then_inc(SM["v13"], 1)
                v.wait_ge(SM["a23"], 2 * g5 + 1)
                v.tensor_mul(r2, r0, r0)
                v.drain()
                v.tensor_sub(r1, r1, r2)
                v.drain().then_inc(SM["v23"], 1)
                v.wait_ge(SM["a23"], 2 * g5 + 2)
                if g5 >= 1:
                    v.wait_ge(SM["p_ms"], 18 * (g5 - 1) + 2)
                v.reciprocal(r3v, r2)
                v.drain()
                v.tensor_mul(r4[0:1, :], r0, r3v)
                v.drain().then_inc(SM["v23"], 1)
                v.wait_ge(SM["p_ms"], base + 1)
                v.tensor_copy(mean_b[:], ps_ms[0][:])
                v.drain().then_inc(SM["v_st"], 1)
                v.wait_ge(SM["p_ms"], base + 2)
                v.tensor_copy(rstd_b[:], ps_ms[1][:])
                v.drain().then_inc(SM["v_st"], 1)
                if g5 >= 1:
                    v.wait_ge(SM["p_gu"], (g5 * NM) * NI)
                    v.wait_ge(SM["p_ms"], 18 * (g5 - 1) + 8)
                for c in range(8):
                    v.tensor_sub(sg[:, 0, :], x_sb[:, c, GSL], mean_b[:])
                    v.drain()
                    v.tensor_mul(h2s[:, c, :], sg[:, 0, :], rstd_b[:])
                    v.drain()
                v.drain().then_inc(SM["v_h2"], 1)
                LG_CNT = [3, 4, 6, 8]
                CWT_CNT = [5, 7, 9, 10]
                for t1 in range(4):
                    bank = t1 % 2
                    v.wait_ge(SM["p_ms"], base + LG_CNT[t1])
                    v.tensor_copy(lg_sb[:], ps_ms[bank][:, 0:8])
                    v.drain()
                    v.reduce_max(gcl[:, 0:1], lg_sb[:], AX.X, negate=True)
                    v.drain().then_inc(SM["v_gt1"], 1)
                    v.wait_ge(SM["a_g"], 4 * g5 + t1 + 1)
                    v.reduce_sum(gcl[:, 1:2], e_g[:], AX.X)
                    v.drain()
                    v.reciprocal(gcl[:, 2:3], gcl[:, 1:2])
                    v.drain()
                    v.tensor_scalar_mul(lg_sb[:], e_g[:], gcl[:, 2:3])
                    v.drain()
                    v.reduce_max(gcl[:, 3:4], lg_sb[:], AX.X)
                    v.drain()
                    v.tensor_scalar(k_g[:], lg_sb[:], gcl[:, 3:4], None, ALU.is_ge)
                    v.drain()
                    v.tensor_sub(e_g[:], lg_sb[:], k_g[:])
                    v.drain()
                    v.reduce_max(gcl[:, 4:5], e_g[:], AX.X)
                    v.drain()
                    v.tensor_tensor(gcl[:, 5:6], gcl[:, 3:4], gcl[:, 4:5], ALU.add)
                    v.drain()
                    v.reciprocal(gcl[:, 6:7], gcl[:, 5:6])
                    v.drain()
                    v.tensor_scalar(k_g[:], lg_sb[:], gcl[:, 4:5], None, ALU.is_ge)
                    v.drain()
                    if t1 >= 1:
                        v.wait_ge(SM["p_ms"], base + CWT_CNT[t1 - 1])
                    v.scalar_tensor_tensor(cw_g[:], lg_sb[:], gcl[:, 6:7], k_g[:],
                                           ALU.mult, ALU.mult)
                    v.drain().then_inc(SM["v_gt2"], 1)
                    v.wait_ge(SM["p_ms"], base + CWT_CNT[t1])
                    v.tensor_copy(cwt[0:8, t1 * 128:(t1 + 1) * 128],
                                  ps_ms[bank][0:8, 0:128])
                    v.drain().then_inc(SM["v_cw"], 1)
                for m in range(NM):
                    GM = g5 * NM + m
                    if m >= 1:
                        v.wait_ge(SM["p_ms"], base + 10 + m)
                        v.tensor_copy(cbc[:], ps_ms[(m - 1) % 2][:])
                        v.drain().then_inc(SM["v_cb"], 1)
                    for i in range(NI):
                        GP = GM * NI + i
                        v.wait_ge(SM["a_sil"], GP + 1)
                        if i == 0 and GM >= 1:
                            v.wait_ge(SM["p_y"], GM * 8)
                        v.tensor_mul(a_t[:, i, :], sg[:, GP % 2, :],
                                     ps_gu[(GP % 2) * 2 + 1][:]).then_inc(SM["v_a"], 1)
                    for hc in range(8):
                        GY = GM * 8 + hc
                        v.wait_ge(SM["p_y"], GY + 1)
                        if m == 0:
                            v.tensor_add(x_sb[:, hc, GSL], x_sb[:, hc, GSL],
                                         ps_y[GY % 2][:]).then_inc(SM["v_y"], 1)
                        else:
                            v.tensor_mul(mean_b[:], ps_y[GY % 2][:], cbc[:])
                            v.drain()
                            v.tensor_add(x_sb[:, hc, GSL], x_sb[:, hc, GSL],
                                         mean_b[:]).then_inc(SM["v_y"], 1)

        @block.gpsimd
        def _(g):
            g.dma_start(out=x_sb[:], in_=x_in).then_inc(SM["s_x"], 16)
            g.dma_start(out=idb[:], in_=const_in[:, C_ID:C_ID + 128]).then_inc(SM["s_kb"], 16)
            g.dma_start(out=maskb[:], in_=const_in[:, C_MASK:C_MASK + 512]).then_inc(SM["s_kb"], 16)
            g.dma_start(out=onescol_b[:], in_=const_in[:, C_ONESCOL:C_ONESCOL + 1],
                        allow_slow_non_contiguous=True).then_inc(SM["s_kb"], 16)
            g.dma_start(out=selb[:], in_=const_in[0:8, C_SEL:C_SEL + 1024]).then_inc(SM["s_kb"], 16)
            g.dma_start(out=gimgb[:], in_=gate_in).then_inc(SM["s_kb"], 16)
            if PHASE_LIMIT < 3:
                return
            emit_precast_loads(g)

        @block.sync
        def _(sy):
            sy.dma_start(out=kf[:], in_=const_in[:, 0:392]).then_inc(SM["s_k"], 16)
            if PHASE_LIMIT >= 3:
                emit_precast_stores(sy)
            if PHASE_LIMIT < 3:
                if PHASE_LIMIT == 0:
                    sy.wait_ge(SM["s_x"], 16)
                    sy.wait_ge(SM["s_kb"], 80)
                elif PHASE_LIMIT == 1:
                    if P1_LEVEL == 1:
                        sy.wait_ge(SM["v1"], 64)
                    elif P1_LEVEL == 2:
                        sy.wait_ge(SM["pxs"], 128)
                    elif P1_LEVEL == 3:
                        sy.wait_ge(SM["v2"], 16)
                    else:
                        sy.wait_ge(SM["v3"], 8)
                else:
                    sy.wait_ge(SM["v_xup"], 128)
                nso = 0
                for g5 in range(NG5):
                    for c in range(8):
                        sy.dma_start(out=y_out[:, c, G5SL[g5]],
                                     in_=x_sb[:, c, G5SL[g5]]).then_inc(SM["s_out"], 16)
                        nso += 1
                sy.wait_ge(SM["s_out"], 16 * nso)
                return
            def pc_wait(chunk_idx):
                sy.wait_ge(SM["s_st0"], 16 * PC_CUM[chunk_idx])

            for g5 in range(NG5):
                def issue_wgu(m_, i):
                    GP = (g5 * NM + m_) * NI + i
                    if g5 == 0:
                        pc_wait(m_ * 38 + i)
                    if GP == 0:
                        sy.wait_ge(SM["v_xup"], 128)
                    if GP >= 3:
                        sy.wait_ge(SM["p_gu"], GP - 2)
                    sy.dma_start(out=wgu_buf[:, GP % 3, :],
                                 in_=wgu_bf[m_, :, i, :]).then_inc(
                                     SM[f"s_wg{GP % 3}"], 16)

                for m in range(NM):
                    GM = g5 * NM + m
                    if m == 0:
                        issue_wgu(m, 0)
                    for i in range(1, NI):
                        issue_wgu(m, i)
                    if m < NM - 1:
                        issue_wgu(m + 1, 0)
                    for hc in range(8):
                        for half in range(2):
                            WH = 2 * (GM * 8 + hc) + half
                            if g5 == 0:
                                pc_wait(m * 38 + 22 + 2 * hc + half)
                            if WH >= 3:
                                sy.wait_ge(SM["p_y"], (WH - 3) // 2 + 1)
                            sy.dma_start(out=wd_buf[:, WH % 3, :],
                                         in_=wd_bf[m, :, hc, half * 1408:(half + 1) * 1408]
                                         ).then_inc(SM[f"s_wd{WH % 3}"], 16)
                    if m == 0 and g5 >= 1:
                        pg = g5 - 1
                        sy.wait_ge(SM["v_y"], (pg * NM + NM) * 8)
                        for c in range(8):
                            sy.dma_start(out=y_out[:, c, G5SL[pg]],
                                         in_=x_sb[:, c, G5SL[pg]]).then_inc(SM["s_out"], 16)
            sy.wait_ge(SM["v_y"], (7 * NM + NM) * 8)
            for c in range(8):
                sy.dma_start(out=y_out[:, c, G5SL[7]],
                             in_=x_sb[:, c, G5SL[7]]).then_inc(SM["s_out"], 16)
            sy.wait_ge(SM["s_out"], 16 * 64)

    es.close()
    return nc


_BUILT = {}


def _get_nc():
    if "nc" not in _BUILT:
        _BUILT["nc"] = build_nc()
    return _BUILT["nc"]


def _host_images(L0_attn_x, L0_ffn_gate_w, L0_ffn_base_wgu, L0_ffn_base_wd,
                 L0_ffn_exp_wgu, L0_ffn_exp_wd):
    x = np.asarray(L0_attn_x, dtype=np.float32)
    gate_w = np.asarray(L0_ffn_gate_w, dtype=np.float32)
    base_wgu = np.asarray(L0_ffn_base_wgu, dtype=np.float32)
    base_wd = np.asarray(L0_ffn_base_wd, dtype=np.float32)
    exp_wgu = np.asarray(L0_ffn_exp_wgu, dtype=np.float32)
    exp_wd = np.asarray(L0_ffn_exp_wd, dtype=np.float32)

    x_imgs = []
    for b in range(NCORE):
        xi = np.ascontiguousarray(
            x[b].T.reshape(NCH, 128, S).transpose(1, 0, 2))
        x_imgs.append(xi)

    wgu_img = np.empty((NM, 128, NI, 2048), np.float32)
    wd_img = np.empty((NM, 128, 8, 2816), np.float32)
    for m in range(NM):
        wgu = base_wgu if m == 0 else exp_wgu[m - 1]
        wd = base_wd if m == 0 else exp_wd[m - 1]
        w4 = wgu.reshape(8, 128, 5632)
        gpart = w4[:, :, :2816].reshape(8, 128, NI, 128)
        upart = w4[:, :, 2816:].reshape(8, 128, NI, 128)
        cat = np.concatenate([gpart, upart], axis=3)
        wgu_img[m] = cat.transpose(1, 2, 0, 3).reshape(128, NI, 2048)
        wd4 = wd.reshape(NI, 128, 8, 128)
        wd_img[m] = wd4.transpose(1, 2, 0, 3).reshape(128, 8, 2816)

    gate_img = np.ascontiguousarray(
        gate_w.reshape(8, 128, 8).transpose(1, 0, 2).reshape(128, 64))

    cimg = np.zeros((128, C_W), np.float32)
    cimg[:, C_ID:C_ID + 128] = np.eye(128, dtype=np.float32)
    kk, jj = np.meshgrid(np.arange(128), np.arange(128), indexing="ij")
    cimg[:, C_MASK:C_MASK + 128] = np.where(kk > jj, -1000.0, 0.0)
    cimg[:, C_ONESCOL] = 1.0
    cimg[0, C_ONESROW:C_ONESROW + 128] = 1.0
    cimg[0, C_EPS] = 1e-5
    cimg[0, C_NEGHALF] = -0.5
    cimg[0, C_HALF] = 0.5
    sel = np.zeros((8, 1024), np.float32)
    for e in range(8):
        sel[e, e * 128:(e + 1) * 128] = 1.0
    cimg[0:8, C_SEL:C_SEL + 1024] = sel

    return x_imgs, wgu_img, wd_img, gate_img, cimg


def kernel(L0_attn_x, L0_ffn_gate_w, L0_ffn_base_wgu, L0_ffn_base_wd,
           L0_ffn_exp_wgu, L0_ffn_exp_wd):
    x_imgs, wgu_img, wd_img, gate_img, cimg = _host_images(
        L0_attn_x, L0_ffn_gate_w, L0_ffn_base_wgu, L0_ffn_base_wd,
        L0_ffn_exp_wgu, L0_ffn_exp_wd)
    nc = _get_nc()
    in_maps = [{
        "x_img": x_imgs[b],
        "wgu_img": wgu_img,
        "wd_img": wd_img,
        "gate_img": gate_img,
        "const_img": cimg,
    } for b in range(NCORE)]
    res = run_bass_kernel_spmd(nc, in_maps, core_ids=list(range(NCORE)))
    out = np.empty((NCORE, S, D), np.float32)
    for b in range(NCORE):
        yi = np.asarray(res.results[b]["y_img"], np.float32)
        out[b] = yi.transpose(1, 0, 2).reshape(D, S).T
    return out


# version 18
# speedup vs baseline: 1.0377x; 1.0377x over previous
import numpy as np
import ml_dtypes
from contextlib import ExitStack

import concourse.bass as bass
import concourse.mybir as mybir
from concourse.bass_utils import run_bass_kernel_spmd

F32 = mybir.dt.float32
BF16 = mybir.dt.bfloat16
AF = mybir.ActivationFunctionType
ALU = mybir.AluOpType
AX = mybir.AxisListType

NCORE = 8
S = 4096
D = 1024
NCH = 8
NH = 16
NQ = 8
NI = 22
NM = 9
NG5 = 8
PHASE_LIMIT = 3
P1_LEVEL = 4

def kbs_of(Q):
    return 4 * (Q + 1)

TPH = sum(kbs_of(Q) for Q in range(NQ))
QB = [0]
for _q in range(NQ):
    QB.append(QB[-1] + kbs_of(_q))

def gti(h, Q, kb):
    return h * TPH + QB[Q] + kb

NTILE = NH * TPH

C_ID = 0
C_TRI = 128
C_ONESCOL = 256
C_ONESROW = 257
C_EPS = 385
C_NEGHALF = 386
C_HALF = 387
C_SEL = 392
C_MASK = C_SEL + 1024
C_W = C_MASK + 512


def build_nc():
    nc = bass.Bass("TRN2", target_bir_lowering=False)

    x_in = nc.dram_tensor("x_img", [128, NCH, S], F32, kind="ExternalInput").ap()
    y_out = nc.dram_tensor("y_img", [128, NCH, S], F32, kind="ExternalOutput").ap()
    wgu_in = nc.dram_tensor("wgu_img", [NM, 128, NI, 2048], F32, kind="ExternalInput").ap()
    wd_in = nc.dram_tensor("wd_img", [NM, 128, 8, 2816], F32, kind="ExternalInput").ap()
    gate_in = nc.dram_tensor("gate_img", [128, 64], F32, kind="ExternalInput").ap()
    const_in = nc.dram_tensor("const_img", [128, C_W], F32, kind="ExternalInput").ap()
    wgu_bf = nc.dram_tensor("wgu_bf", [NM, 128, NI, 2048], BF16, kind="Internal").ap()
    wd_bf = nc.dram_tensor("wd_bf", [NM, 128, 8, 2816], BF16, kind="Internal").ap()

    es = ExitStack()
    sb = es.enter_context

    kf = sb(nc.sbuf_tensor("kf", [128, 260], F32))
    idb = sb(nc.sbuf_tensor("idb", [128, 128], BF16))
    maskb = sb(nc.sbuf_tensor("maskb", [128, 512], BF16))
    onescol_b = sb(nc.sbuf_tensor("onescol_b", [128, 1], BF16))
    selb = sb(nc.sbuf_tensor("selb", [8, 1024], BF16))
    gimgb = sb(nc.sbuf_tensor("gimgb", [128, 64], BF16))
    x_sb = sb(nc.sbuf_tensor("x_sb", [128, NCH, S], F32))
    bounce = sb(nc.sbuf_tensor("bounce", [128, 3, 512], BF16))
    xb = sb(nc.sbuf_tensor("xb", [128, 2, 512], BF16))
    xsq = sb(nc.sbuf_tensor("xsq", [128, 2, 512], BF16))
    RA = sb(nc.sbuf_tensor("RA", [33, 512], F32))
    RB = sb(nc.sbuf_tensor("RB", [33, 512], F32))
    r0 = RA[0:1, :]
    r1 = RA[32:33, :]
    r3v = RB[0:1, :]
    r2 = RB[32:33, :]
    mr_col = sb(nc.sbuf_tensor("mr_col", [128, 32], F32))
    r_col = sb(nc.sbuf_tensor("r_col", [128, 32], F32))

    id_f = kf[:, 0:128]
    id1 = kf[0:1, 0:1]
    ones1x128f = kf[0:1, 129:257]
    ones1x64f = kf[0:1, 129:193]
    eps_cell = kf[0:1, 257:258]
    neghalf_cell = kf[0:1, 258:259]
    half_cell = kf[0:1, 259:260]

    esB = ExitStack()
    r4 = esB.enter_context(nc.sbuf_tensor("r4", [1, 512], F32))
    h_tm = [esB.enter_context(nc.sbuf_tensor(f"h_tm{i}", [128, 32, 130], BF16)) for i in range(2)]
    h65 = [esB.enter_context(nc.sbuf_tensor(f"h65_{i}", [65, S], BF16)) for i in range(2)]
    aug = [esB.enter_context(nc.sbuf_tensor(f"aug{i}", [65, 512], BF16)) for i in range(4)]
    n2row = [esB.enter_context(nc.sbuf_tensor(f"n2row{i}", [1, 512], BF16)) for i in range(2)]
    sqt = [esB.enter_context(nc.sbuf_tensor(f"sqt{i}", [64, 512], BF16)) for i in range(2)]
    pt = [esB.enter_context(nc.sbuf_tensor(f"pt{i}", [128, 512], BF16)) for i in range(4)]
    o_sb = [esB.enter_context(nc.sbuf_tensor(f"o_sb{i}", [65, 512], F32)) for i in range(2)]
    t1t = esB.enter_context(nc.sbuf_tensor("t1t", [128, 512], F32))
    rlr = [esB.enter_context(nc.sbuf_tensor(f"rlr{i}", [1, 512], F32)) for i in range(2)]
    cg = esB.enter_context(nc.sbuf_tensor("cg", [1, 8], F32))
    esB.close()

    esC = ExitStack()
    a_t = esC.enter_context(nc.sbuf_tensor("a_t", [128, NI, 512], BF16))
    h2s = esC.enter_context(nc.sbuf_tensor("h2s", [128, NCH, 512], BF16))
    wgu_buf = esC.enter_context(nc.sbuf_tensor("wgu_buf", [128, 3, 2048], BF16))
    wd_buf = esC.enter_context(nc.sbuf_tensor("wd_buf", [128, 4, 1408], BF16))
    sg = esC.enter_context(nc.sbuf_tensor("sg", [128, 2, 512], F32))
    cwt = esC.enter_context(nc.sbuf_tensor("cwt", [8, 512], BF16))
    cbc = esC.enter_context(nc.sbuf_tensor("cbc", [128, 512], F32))
    mean_b = esC.enter_context(nc.sbuf_tensor("mean_b", [128, 512], F32))
    rstd_b = esC.enter_context(nc.sbuf_tensor("rstd_b", [128, 512], F32))
    lg_sb = esC.enter_context(nc.sbuf_tensor("lg_sb", [128, 8], F32))
    e_g = esC.enter_context(nc.sbuf_tensor("e_g", [128, 8], F32))
    k_g = esC.enter_context(nc.sbuf_tensor("k_g", [128, 8], F32))
    cw_g = esC.enter_context(nc.sbuf_tensor("cw_g", [128, 8], F32))
    gcl = esC.enter_context(nc.sbuf_tensor("gcl", [128, 8], F32))
    esC.close()

    esP1 = ExitStack()
    ps_m1 = esP1.enter_context(nc.psum_tensor("ps_m1", [1, 512], F32))
    ps_m2 = esP1.enter_context(nc.psum_tensor("ps_m2", [1, 512], F32))
    ps_tc = esP1.enter_context(nc.psum_tensor("ps_tc", [128, 8], F32))
    esP1.close()
    esP2 = ExitStack()
    ps_s = [esP2.enter_context(nc.psum_tensor(f"ps_s{i}", [128, 512], F32)) for i in range(3)]
    ps_o = [esP2.enter_context(nc.psum_tensor(f"ps_o{i}", [65, 512], F32)) for i in range(2)]
    ps_bx = esP2.enter_context(nc.psum_tensor("ps_bx", [128, 512], F32))
    ps_bh = esP2.enter_context(nc.psum_tensor("ps_bh", [64, 512], BF16))
    ps_nrl = esP2.enter_context(nc.psum_tensor("ps_nrl", [65, 512], F32))
    esP2.close()
    esP3 = ExitStack()
    ps_gu = [esP3.enter_context(nc.psum_tensor(f"ps_gu{i}", [128, 512], F32)) for i in range(4)]
    ps_y = [esP3.enter_context(nc.psum_tensor(f"ps_y{i}", [128, 512], F32)) for i in range(2)]
    ps_ms = [esP3.enter_context(nc.psum_tensor(f"ps_ms{i}", [128, 512], F32)) for i in range(2)]
    esP3.close()

    sem_names = [
        "s_x", "s_k", "s_kb", "s_ld0", "s_st0",
        "s_wg0", "s_wg1", "s_wg2",
        "s_wd0", "s_wd1", "s_wd2", "s_wd3", "s_out",
        "a1", "v1", "pxs", "a2", "v2", "p1t", "v3",
        "a_sq", "p_n2", "v_aug", "p_s", "a_exp", "p_pv",
        "v_rl", "p_rlb", "v_xup", "p_bx", "v_bx", "p_bh", "v_bh",
        "a13", "v13", "pxs3", "a23", "v23", "v_st", "p_ms", "v_gt1",
        "a_g", "v_gt2", "v_cw", "p_gu", "a_sil", "v_a", "p_y", "v_y",
        "v_cb", "v_h2",
    ]
    SM = {n: sb(nc.semaphore(n)) for n in sem_names}

    def mean_cnt(ci):
        return 2 * ci + 1

    def sq_cnt(ci):
        return 2 * ci + 2

    def head_builds(h):
        out = []
        if h >= 15:
            return out
        if h % 2 == 1:
            p = (h + 1) // 2
            out += [("bx", p, g) for g in range(8)]
        out += [("bh", h + 1, g) for g in range(8)]
        return out

    G5SL = [slice(g5 * 512, (g5 + 1) * 512) for g5 in range(NG5)]

    PC_PIECES = []
    PC_CUM = []
    _cum = 0
    for _m in range(NM):
        for _i in range(NI):
            for _qq in range(4):
                PC_PIECES.append(("wgu", _m, _i, 0, _qq * 512, 512))
            _cum += 4
            PC_CUM.append(_cum)
        for _hc in range(8):
            for _half in range(2):
                for _off, _w in ((0, 512), (512, 512), (1024, 384)):
                    PC_PIECES.append(("wd", _m, _hc, _half, _off, _w))
                _cum += 3
                PC_CUM.append(_cum)

    def _pc_aps(piece):
        kind, m_, a, b, off, w = piece
        if kind == "wgu":
            return (wgu_in[m_, :, a, off:off + w],
                    wgu_bf[m_, :, a, off:off + w], w)
        cs = b * 1408 + off
        return (wd_in[m_, :, a, cs:cs + w],
                wd_bf[m_, :, a, cs:cs + w], w)

    def emit_precast_loads(eng):
        for p, piece in enumerate(PC_PIECES):
            src_ap, _, w = _pc_aps(piece)
            if p >= 3:
                eng.wait_ge(SM["s_st0"], 16 * (p - 2))
            if PHASE_LIMIT >= 2 and 2 * p - 300 > 0:
                eng.wait_ge(SM["a_exp"], min(2 * p - 300, NTILE))
            eng.dma_start(out=bounce[:, p % 3, 0:w],
                          in_=src_ap).then_inc(SM["s_ld0"], 16)

    def emit_precast_stores(eng):
        for p, piece in enumerate(PC_PIECES):
            _, dst_ap, w = _pc_aps(piece)
            eng.wait_ge(SM["s_ld0"], 16 * (p + 1))
            eng.dma_start(out=dst_ap,
                          in_=bounce[:, p % 3, 0:w]).then_inc(SM["s_st0"], 16)

    with nc.Block() as block:

        @block.tensor
        def _(t):
            if PHASE_LIMIT < 1 or P1_LEVEL < 2:
                if PHASE_LIMIT >= 1 and P1_LEVEL >= 2:
                    pass
                if PHASE_LIMIT < 1 or P1_LEVEL < 2:
                    return
            for tt in range(8):
                for c in range(8):
                    ci = 8 * tt + c
                    t.wait_ge(SM["a1"], ci + 1)
                    if ci == 0:
                        t.wait_ge(SM["s_kb"], 16 * 5)
                        t.wait_ge(SM["s_k"], 32)
                    if tt > 0 and c == 0 and P1_LEVEL >= 3:
                        t.wait_ge(SM["a2"], 2 * (tt - 1) + 1)
                    t.matmul(ps_m1[:], onescol_b[:], xb[:, c % 2, :],
                             start=(c == 0), stop=(c == 7),
                             skip_group_check=True).then_inc(SM["pxs"], 1)
                    t.wait_ge(SM["v1"], ci + 1)
                    t.matmul(ps_m2[:], onescol_b[:], xsq[:, c % 2, :],
                             start=(c == 0), stop=(c == 7),
                             skip_group_check=True).then_inc(SM["pxs"], 1)
                if P1_LEVEL < 4:
                    continue
                t.wait_ge(SM["v2"], 2 * tt + 2)
                if tt > 0:
                    t.wait_ge(SM["v3"], tt)
                for j in range(4):
                    t.transpose(ps_tc[:, j:j + 1], r4[0:1, j * 128:(j + 1) * 128], id1)
                for j in range(4):
                    mmtc = t.transpose(ps_tc[:, 4 + j:5 + j], r3v[0:1, j * 128:(j + 1) * 128], id1)
                mmtc.then_inc(SM["p1t"], 1)

            if PHASE_LIMIT < 2:
                return
            def pe_build(kind, pidx, g):
                if kind == "bx":
                    if 8 * pidx + g >= 1:
                        t.wait_ge(SM["v_bx"], 8 * pidx + g)
                    for j in range(4):
                        tc = g * 4 + j
                        mm = t.transpose(ps_bx[:, j * 128:(j + 1) * 128],
                                         x_sb[:, pidx, tc * 128:(tc + 1) * 128], id_f)
                    mm.then_inc(SM["p_bx"], 1)
                else:
                    hb = pidx
                    t.wait_ge(SM["v_bh"], 8 * hb + g)
                    t.wait_ge(SM["v_bx"], 8 * (hb // 2) + g + 1)
                    for j in range(4):
                        tc = g * 4 + j
                        mm = t.transpose(ps_bh[0:64, j * 128:(j + 1) * 128],
                                         h_tm[(hb // 2) % 2][:, tc, 65 * (hb % 2):65 * (hb % 2) + 64],
                                         idb[:])
                    mm.then_inc(SM["p_bh"], 1)

            for g in range(8):
                pe_build("bx", 0, g)
            for g in range(8):
                pe_build("bh", 0, g)

            for h in range(NH):
                pair = h // 2
                builds = head_builds(h)
                bpos = 0
                def pe_n2(Qn):
                    AQn = 8 * h + Qn
                    t.wait_ge(SM["a_sq"], AQn + 1)
                    if AQn >= 1:
                        t.wait_ge(SM["v_aug"], AQn)
                    t.matmul(ps_nrl[64:65, :], onescol_b[0:64, :], sqt[Qn % 2][:],
                             start=True, stop=True,
                             skip_group_check=True).then_inc(SM["p_n2"], 1)

                def emit_pv(Q, kb):
                    T = gti(h, Q, kb)
                    AQ = 8 * h + Q
                    nkb = kbs_of(Q)
                    sk = max(128 * kb - 512 * Q, 0)
                    t.wait_ge(SM["a_exp"], T + 1)
                    if kb == 0:
                        if AQ >= 2:
                            t.wait_ge(SM["v_rl"], AQ - 1)
                        if Q == 0:
                            t.wait_ge(SM["v_bx"], 8 * (pair + 1))
                    t.matmul(ps_o[AQ % 2][0:65, sk:512],
                             h_tm[pair % 2][:, kb, 65 * (h % 2):65 * (h % 2) + 65],
                             pt[T % 4][:, sk:512],
                             start=(kb == 0), stop=(kb == nkb - 1),
                             skip_group_check=True).then_inc(SM["p_pv"], 1)

                pe_n2(0)
                pe_n2(1)
                pend = []
                defer = []
                for Q in range(NQ):
                    AQ = 8 * h + Q
                    nkb = kbs_of(Q)
                    rlb_slot = 3 if Q == 0 else 5
                    for kb in range(nkb):
                        T = gti(h, Q, kb)
                        delta = 128 * kb - 512 * Q
                        sk = max(delta, 0)

                        if kb == rlb_slot and AQ >= 1:
                            pv = AQ - 1
                            t.wait_ge(SM["v_rl"], pv + 1)
                            if pv >= 1:
                                t.wait_ge(SM["v_xup"], pv)
                            t.matmul(ps_nrl[0:64, :], ones1x64f, rlr[pv % 2][:],
                                     start=True, stop=True,
                                     skip_group_check=True).then_inc(SM["p_rlb"], 1)
                        if kb == 0 and Q == 0:
                            t.wait_ge(SM["v_bh"], 8 * (h + 1))
                        if kb == 0:
                            t.wait_ge(SM["v_aug"], AQ + 1)
                        if T >= 3:
                            t.wait_ge(SM["a_exp"], T - 2)
                        diag = delta >= 0
                        mm = t.matmul(ps_s[T % 3][:, sk:512],
                                      h65[h % 2][0:65, kb * 128:(kb + 1) * 128],
                                      aug[Q % 4][0:65, sk:512],
                                      start=True, stop=(not diag),
                                      skip_group_check=True)
                        if diag:
                            mm = t.matmul(ps_s[T % 3][:, sk:512],
                                          idb[:], maskb[:, 0:512 - sk],
                                          start=False, stop=True,
                                          skip_group_check=True)
                        mm.then_inc(SM["p_s"], 1)
                        pend.append((Q, kb))
                        if len(pend) > 3:
                            emit_pv(*pend.pop(0))
                        if kb == 2:
                            for job in defer:
                                job()
                            defer = []
                    if Q + 2 <= 7:
                        defer.append(lambda Qn=Q + 2: pe_n2(Qn))
                    for _ in range(2 if len(builds) > 8 else 1):
                        if bpos < len(builds):
                            b = builds[bpos]; bpos += 1
                            defer.append(lambda b=b: pe_build(*b))
                while pend:
                    emit_pv(*pend.pop(0))
                for job in defer:
                    job()
                defer = []
                while bpos < len(builds):
                    pe_build(*builds[bpos]); bpos += 1
            t.wait_ge(SM["v_rl"], 128)
            t.wait_ge(SM["v_xup"], 127)
            t.matmul(ps_nrl[0:64, :], ones1x64f, rlr[127 % 2][:],
                     start=True, stop=True,
                     skip_group_check=True).then_inc(SM["p_rlb"], 1)

            if PHASE_LIMIT < 3:
                return
            for g5 in range(NG5):
                base = 18 * g5
                for c in range(8):
                    ci = 8 * g5 + c
                    t.wait_ge(SM["a13"], ci + 1)
                    if c == 0 and g5 >= 1:
                        t.wait_ge(SM["v_cb"], 8 * (g5 - 1) + 7)
                        t.wait_ge(SM["v_cb"], 8 * (g5 - 1) + 8)
                    t.matmul(ps_ms[0][0:1, :], onescol_b[:], xb[:, c % 2, :],
                             start=(c == 0), stop=(c == 7),
                             skip_group_check=True).then_inc(SM["pxs3"], 1)
                    t.wait_ge(SM["v13"], ci + 1)
                    t.matmul(ps_ms[1][0:1, :], onescol_b[:], xsq[:, c % 2, :],
                             start=(c == 0), stop=(c == 7),
                             skip_group_check=True).then_inc(SM["pxs3"], 1)
                t.wait_ge(SM["v23"], 2 * g5 + 2)
                t.wait_ge(SM["a23"], 2 * g5 + 1)
                t.matmul(ps_ms[0][:], ones1x128f, r0,
                         start=True, stop=True).then_inc(SM["p_ms"], 1)
                t.matmul(ps_ms[1][:], ones1x128f, r3v,
                         start=True, stop=True).then_inc(SM["p_ms"], 1)
                def pe_lg(t1):
                    bank = t1 % 2
                    if t1 == 0:
                        t.wait_ge(SM["v_st"], 2 * g5 + 1)
                        t.wait_ge(SM["v_h2"], g5 + 1)
                    elif t1 == 1:
                        t.wait_ge(SM["v_st"], 2 * g5 + 2)
                    else:
                        t.wait_ge(SM["v_cw"], 4 * g5 + t1 - 1)
                    for kc in range(8):
                        mm = t.matmul(ps_ms[bank][:, 0:8],
                                      h2s[:, kc, t1 * 128:(t1 + 1) * 128],
                                      gimgb[:, kc * 8:(kc + 1) * 8],
                                      start=(kc == 0), stop=(kc == 7))
                    mm.then_inc(SM["p_ms"], 1)

                def pe_cwt(t1):
                    bank = t1 % 2
                    t.wait_ge(SM["v_gt1"], 4 * g5 + t1 + 1)
                    t.wait_ge(SM["v_gt2"], 4 * g5 + t1 + 1)
                    t.transpose(ps_ms[bank][0:8, 0:128], cw_g[:, 0:8],
                                id_f).then_inc(SM["p_ms"], 1)

                pe_lg(0)
                pe_lg(1)
                pe_cwt(0)
                pe_lg(2)
                pe_cwt(1)
                pe_lg(3)
                pe_cwt(2)
                pe_cwt(3)
                def emit_gu(m_, i):
                    GP = (g5 * NM + m_) * NI + i
                    t.wait_ge(SM[f"s_wg{GP % 3}"], 16 * (GP // 3 + 1))
                    if GP >= 2:
                        t.wait_ge(SM["v_a"], GP - 1)
                    if i == 0:
                        t.wait_ge(SM["v_h2"], g5 + 1)
                    bg = (GP % 2) * 2
                    bu = bg + 1
                    for k in range(8):
                        t.matmul(ps_gu[bg][:], wgu_buf[:, GP % 3, k * 256:k * 256 + 128],
                                 h2s[:, k, :], start=(k == 0), stop=(k == 7))
                    for k in range(8):
                        mm = t.matmul(ps_gu[bu][:], wgu_buf[:, GP % 3, k * 256 + 128:k * 256 + 256],
                                      h2s[:, k, :], start=(k == 0), stop=(k == 7))
                    mm.then_inc(SM["p_gu"], 1)

                for m in range(NM):
                    GM = g5 * NM + m
                    if m >= 1:
                        bank = (m - 1) % 2
                        if m <= 2:
                            t.wait_ge(SM["v_cw"], 4 * g5 + 2 + m)
                        else:
                            t.wait_ge(SM["v_cb"], 8 * g5 + m - 2)
                        t.matmul(ps_ms[bank][:], selb[:, (m - 1) * 128:m * 128],
                                 cwt[0:8, :], start=True, stop=True).then_inc(SM["p_ms"], 1)
                    for i in (range(NI) if m == 0 else range(1, NI)):
                        emit_gu(m, i)
                    if m < NM - 1:
                        emit_gu(m + 1, 0)
                    for hc in range(8):
                        GY = GM * 8 + hc
                        if hc == 0:
                            t.wait_ge(SM["v_a"], GM * NI + NI)
                        if GY >= 2:
                            t.wait_ge(SM["v_y"], GY - 1)
                        for half in range(2):
                            WH = 2 * GY + half
                            t.wait_ge(SM[f"s_wd{WH % 4}"], 16 * (WH // 4 + 1))
                            for il in range(11):
                                ic = half * 11 + il
                                mm = t.matmul(ps_y[GY % 2][:],
                                              wd_buf[:, WH % 4, il * 128:(il + 1) * 128],
                                              a_t[:, ic, :],
                                              start=(ic == 0), stop=(ic == NI - 1))
                        mm.then_inc(SM["p_y"], 1)

        @block.scalar
        def _(s):
            if PHASE_LIMIT < 1:
                return
            for tt in range(8):
                for c in range(8):
                    ci = 8 * tt + c
                    if ci == 0:
                        s.wait_ge(SM["s_x"], 16)
                    if ci >= 2:
                        s.wait_ge(SM["v1"], ci - 1)
                        if P1_LEVEL >= 2:
                            s.wait_ge(SM["pxs"], mean_cnt(ci - 2))
                    s.copy(xb[:, c % 2, :],
                           x_sb[:, c, tt * 512:(tt + 1) * 512]).then_inc(SM["a1"], 1)
                if P1_LEVEL < 3:
                    continue
                s.wait_ge(SM["pxs"], 16 * tt + 16)
                if tt >= 1:
                    s.wait_ge(SM["v2"], 2 * (tt - 1) + 2)
                s.mul(r0, ps_m1[0:1, :], 1.0 / 1024.0)
                s.mul(r1, ps_m2[0:1, :], 1.0 / 1024.0)
                s.drain().then_inc(SM["a2"], 1)
                if ci == 7 and tt == 0:
                    s.wait_ge(SM["s_k"], 32)
                s.wait_ge(SM["v2"], 2 * tt + 1)
                if P1_LEVEL >= 4:
                    s.wait_ge(SM["p1t"], tt)
                s.activation(r2, r1, AF.Sqrt, bias=eps_cell)
                s.drain().then_inc(SM["a2"], 1)

            if PHASE_LIMIT < 2:
                return
            for h in range(NH):
                for Q in range(NQ):
                    nkb = kbs_of(Q)
                    for kb in range(nkb):
                        g = gti(h, Q, kb)
                        sk = max(128 * kb - 512 * Q, 0)
                        s.wait_ge(SM["p_s"], g + 1)
                        if g >= 4:
                            s.wait_ge(SM["p_pv"], g - 3)
                        s.activation(pt[g % 4][:, sk:512], ps_s[g % 3][:, sk:512],
                                     AF.Exp, scale=0.125).then_inc(SM["a_exp"], 1)

            if PHASE_LIMIT < 3:
                return
            for g5 in range(NG5):
                base = 18 * g5
                GSL = G5SL[g5]
                for c in range(8):
                    ci = 8 * g5 + c
                    if ci == 0:
                        s.wait_ge(SM["v_xup"], 128)
                    if ci >= 2:
                        s.wait_ge(SM["v13"], ci - 1)
                        s.wait_ge(SM["pxs3"], mean_cnt(ci - 2))
                    s.copy(xb[:, c % 2, :], x_sb[:, c, GSL]).then_inc(SM["a13"], 1)
                s.wait_ge(SM["pxs3"], 16 * g5 + 16)
                if g5 >= 1:
                    s.wait_ge(SM["v23"], 2 * (g5 - 1) + 2)
                    s.wait_ge(SM["p_ms"], 18 * (g5 - 1) + 2)
                s.mul(r0, ps_ms[0][0:1, :], 1.0 / 1024.0)
                s.mul(r1, ps_ms[1][0:1, :], 1.0 / 1024.0)
                s.drain().then_inc(SM["a23"], 1)
                s.wait_ge(SM["v23"], 2 * g5 + 1)
                s.activation(r2, r1, AF.Sqrt, bias=eps_cell)
                s.drain().then_inc(SM["a23"], 1)
                for t1 in range(4):
                    s.wait_ge(SM["v_gt1"], 4 * g5 + t1 + 1)
                    s.activation(e_g[:], lg_sb[:], AF.Exp, bias=gcl[:, 0:1])
                    s.drain().then_inc(SM["a_g"], 1)
                for m in range(NM):
                    GM = g5 * NM + m
                    for i in range(NI):
                        GP = GM * NI + i
                        s.wait_ge(SM["p_gu"], GP + 1)
                        if GP >= 2:
                            s.wait_ge(SM["v_a"], GP - 1)
                        s.activation(sg[:, GP % 2, :], ps_gu[(GP % 2) * 2][:],
                                     AF.Silu).then_inc(SM["a_sil"], 1)

        @block.vector
        def _(v):
            if PHASE_LIMIT < 1:
                return
            for tt in range(8):
                for c in range(8):
                    ci = 8 * tt + c
                    v.wait_ge(SM["a1"], ci + 1)
                    if ci >= 2 and P1_LEVEL >= 2:
                        v.wait_ge(SM["pxs"], sq_cnt(ci - 2))
                    v.tensor_mul(xsq[:, c % 2, :], xb[:, c % 2, :],
                                 xb[:, c % 2, :]).then_inc(SM["v1"], 1)
                if P1_LEVEL < 3:
                    continue
                v.wait_ge(SM["a2"], 2 * tt + 1)
                v.tensor_mul(r2, r0, r0)
                v.drain()
                v.tensor_sub(r1, r1, r2)
                v.drain().then_inc(SM["v2"], 1)
                v.wait_ge(SM["a2"], 2 * tt + 2)
                if tt >= 1 and P1_LEVEL >= 4:
                    v.wait_ge(SM["p1t"], tt)
                v.reciprocal(r3v, r2)
                v.drain()
                v.tensor_mul(r4[0:1, :], r0, r3v)
                v.drain().then_inc(SM["v2"], 1)
                if P1_LEVEL < 4:
                    continue
                v.wait_ge(SM["p1t"], tt + 1)
                v.tensor_copy(mr_col[:, 4 * tt:4 * tt + 4], ps_tc[:, 0:4])
                v.tensor_copy(r_col[:, 4 * tt:4 * tt + 4], ps_tc[:, 4:8])
                v.drain().then_inc(SM["v3"], 1)

            if PHASE_LIMIT < 2:
                return
            v.wait_ge(SM["s_k"], 32)
            v.memset(h65[0][64:65, :], 1.0)
            v.memset(h65[1][64:65, :], 1.0)
            v.drain()

            def dve_build(kind, pidx, g):
                if kind == "bx":
                    v.wait_ge(SM["p_bx"], 8 * pidx + g + 1)
                    v.wait_ge(SM["v3"], g + 1)
                    if g == 0:
                        if pidx >= 2:
                            v.wait_ge(SM["p_pv"], TPH * (2 * pidx - 2))
                        v.memset(h_tm[pidx % 2][:, :, 64:65], 1.0)
                        v.memset(h_tm[pidx % 2][:, :, 129:130], 1.0)
                    for j in range(4):
                        tc = g * 4 + j
                        v.tensor_scalar(h_tm[pidx % 2][:, tc, 0:64],
                                        ps_bx[:, j * 128:j * 128 + 64],
                                        r_col[:, tc:tc + 1], mr_col[:, tc:tc + 1],
                                        ALU.mult, ALU.subtract)
                        v.tensor_scalar(h_tm[pidx % 2][:, tc, 65:129],
                                        ps_bx[:, j * 128 + 64:(j + 1) * 128],
                                        r_col[:, tc:tc + 1], mr_col[:, tc:tc + 1],
                                        ALU.mult, ALU.subtract)
                    v.drain().then_inc(SM["v_bx"], 1)
                else:
                    hb = pidx
                    v.wait_ge(SM["p_bh"], 8 * hb + g + 1)
                    if g == 0 and hb >= 2:
                        v.wait_ge(SM["p_s"], TPH * (hb - 1))
                        v.wait_ge(SM["a_sq"], 8 * (hb - 1))
                    for j in range(4):
                        tc = g * 4 + j
                        v.tensor_copy(h65[hb % 2][0:64, tc * 128:(tc + 1) * 128],
                                      ps_bh[0:64, j * 128:(j + 1) * 128])
                    v.drain().then_inc(SM["v_bh"], 1)

            def dve_sq(h, Qn):
                AQn = 8 * h + Qn
                if Qn == 0:
                    v.wait_ge(SM["v_bh"], 8 * (h + 1))
                if AQn >= 2:
                    v.wait_ge(SM["p_n2"], AQn - 1)
                v.tensor_mul(sqt[Qn % 2][:],
                             h65[h % 2][0:64, Qn * 512:(Qn + 1) * 512],
                             h65[h % 2][0:64, Qn * 512:(Qn + 1) * 512]
                             ).then_inc(SM["a_sq"], 1)

            def dve_pre(h, Q):
                AQ = 8 * h + Q
                v.wait_ge(SM["p_n2"], AQ + 1)
                if Q == 0:
                    v.wait_ge(SM["p_s"], TPH * h)
                v.tensor_copy(n2row[Q % 2][0:1, :], ps_nrl[64:65, :])
                v.drain()
                v.reduce_max(cg[:, 0:1], n2row[Q % 2][0:1, :], AX.X, negate=True)
                v.drain()
                if Q == 0:
                    v.tensor_copy(cg[:, 1 + (Q % 2):2 + (Q % 2)], cg[:, 0:1])
                else:
                    v.tensor_tensor(cg[:, 1 + (Q % 2):2 + (Q % 2)],
                                    cg[:, 1 + ((Q - 1) % 2):2 + ((Q - 1) % 2)],
                                    cg[:, 0:1], ALU.min)
                v.drain()
                v.tensor_scalar_mul(cg[:, 3:4], cg[:, 1 + (Q % 2):2 + (Q % 2)], half_cell)
                v.drain()
                v.tensor_scalar(aug[Q % 4][64:65, :], n2row[Q % 2][0:1, :],
                                neghalf_cell, cg[:, 3:4], ALU.mult, ALU.add)
                v.tensor_copy(aug[Q % 4][0:64, :], h65[h % 2][0:64, Q * 512:(Q + 1) * 512])
                v.drain().then_inc(SM["v_aug"], 1)

            def dve_finB(aq):
                hh, QQ = aq // 8, aq % 8
                rowbase = 64 * (hh % 2)
                c = hh // 2
                v.wait_ge(SM["p_rlb"], aq + 1)
                v.tensor_mul(t1t[rowbase:rowbase + 64, :], o_sb[aq % 2][0:64, :],
                             ps_nrl[0:64, :])
                v.drain()
                v.tensor_add(x_sb[rowbase:rowbase + 64, c, QQ * 512:(QQ + 1) * 512],
                             x_sb[rowbase:rowbase + 64, c, QQ * 512:(QQ + 1) * 512],
                             t1t[rowbase:rowbase + 64, :])
                v.drain().then_inc(SM["v_xup"], 1)

            for g in range(8):
                dve_build("bx", 0, g)
            for g in range(8):
                dve_build("bh", 0, g)

            for h in range(NH):
                builds = head_builds(h)
                bpos = 0
                dve_sq(h, 0)
                dve_pre(h, 0)
                dve_sq(h, 1)
                dve_pre(h, 1)
                for Q in range(NQ):
                    AQ = 8 * h + Q
                    v.wait_ge(SM["p_pv"], h * TPH + QB[Q + 1])
                    v.tensor_copy(o_sb[AQ % 2][:], ps_o[AQ % 2][0:65, :])
                    v.drain()
                    v.reciprocal(rlr[AQ % 2][0:1, :], o_sb[AQ % 2][64:65, :])
                    v.drain().then_inc(SM["v_rl"], 1)
                    if AQ >= 1:
                        dve_finB(AQ - 1)
                    if Q + 2 <= 7:
                        dve_sq(h, Q + 2)
                    if Q + 2 <= 7:
                        dve_pre(h, Q + 2)
                    if bpos < len(builds):
                        dve_build(*builds[bpos]); bpos += 1
                    if bpos < len(builds) and len(builds) > 8:
                        dve_build(*builds[bpos]); bpos += 1
                while bpos < len(builds):
                    dve_build(*builds[bpos]); bpos += 1
            dve_finB(127)

            if PHASE_LIMIT < 3:
                return
            for g5 in range(NG5):
                base = 18 * g5
                GSL = G5SL[g5]
                for c in range(8):
                    ci = 8 * g5 + c
                    v.wait_ge(SM["a13"], ci + 1)
                    if ci >= 2:
                        v.wait_ge(SM["pxs3"], sq_cnt(ci - 2))
                    v.tensor_mul(xsq[:, c % 2, :], xb[:, c % 2, :],
                                 xb[:, c % 2, :]).then_inc(SM["v13"], 1)
                v.wait_ge(SM["a23"], 2 * g5 + 1)
                v.tensor_mul(r2, r0, r0)
                v.drain()
                v.tensor_sub(r1, r1, r2)
                v.drain().then_inc(SM["v23"], 1)
                v.wait_ge(SM["a23"], 2 * g5 + 2)
                if g5 >= 1:
                    v.wait_ge(SM["p_ms"], 18 * (g5 - 1) + 2)
                v.reciprocal(r3v, r2)
                v.drain().then_inc(SM["v23"], 1)
                v.wait_ge(SM["p_ms"], base + 1)
                v.tensor_copy(mean_b[:], ps_ms[0][:])
                v.drain().then_inc(SM["v_st"], 1)
                v.wait_ge(SM["p_ms"], base + 2)
                v.tensor_copy(rstd_b[:], ps_ms[1][:])
                v.drain().then_inc(SM["v_st"], 1)
                if g5 >= 1:
                    v.wait_ge(SM["p_gu"], (g5 * NM) * NI)
                    v.wait_ge(SM["p_ms"], 18 * (g5 - 1) + 8)
                for c in range(8):
                    v.tensor_sub(sg[:, 0, :], x_sb[:, c, GSL], mean_b[:])
                    v.drain()
                    v.tensor_mul(h2s[:, c, :], sg[:, 0, :], rstd_b[:])
                    v.drain()
                v.drain().then_inc(SM["v_h2"], 1)
                LG_CNT = [3, 4, 6, 8]
                CWT_CNT = [5, 7, 9, 10]
                for t1 in range(4):
                    bank = t1 % 2
                    v.wait_ge(SM["p_ms"], base + LG_CNT[t1])
                    v.tensor_copy(lg_sb[:], ps_ms[bank][:, 0:8])
                    v.drain()
                    v.reduce_max(gcl[:, 0:1], lg_sb[:], AX.X, negate=True)
                    v.drain().then_inc(SM["v_gt1"], 1)
                    v.wait_ge(SM["a_g"], 4 * g5 + t1 + 1)
                    v.reduce_sum(gcl[:, 1:2], e_g[:], AX.X)
                    v.drain()
                    v.reciprocal(gcl[:, 2:3], gcl[:, 1:2])
                    v.drain()
                    v.tensor_scalar_mul(lg_sb[:], e_g[:], gcl[:, 2:3])
                    v.drain()
                    v.reduce_max(gcl[:, 3:4], lg_sb[:], AX.X)
                    v.drain()
                    v.tensor_scalar(k_g[:], lg_sb[:], gcl[:, 3:4], None, ALU.is_ge)
                    v.drain()
                    v.tensor_sub(e_g[:], lg_sb[:], k_g[:])
                    v.drain()
                    v.reduce_max(gcl[:, 4:5], e_g[:], AX.X)
                    v.drain()
                    v.tensor_tensor(gcl[:, 5:6], gcl[:, 3:4], gcl[:, 4:5], ALU.add)
                    v.drain()
                    v.reciprocal(gcl[:, 6:7], gcl[:, 5:6])
                    v.drain()
                    v.tensor_scalar(k_g[:], lg_sb[:], gcl[:, 4:5], None, ALU.is_ge)
                    v.drain()
                    if t1 >= 1:
                        v.wait_ge(SM["p_ms"], base + CWT_CNT[t1 - 1])
                    v.scalar_tensor_tensor(cw_g[:], lg_sb[:], gcl[:, 6:7], k_g[:],
                                           ALU.mult, ALU.mult)
                    v.drain().then_inc(SM["v_gt2"], 1)
                    v.wait_ge(SM["p_ms"], base + CWT_CNT[t1])
                    v.tensor_copy(cwt[0:8, t1 * 128:(t1 + 1) * 128],
                                  ps_ms[bank][0:8, 0:128])
                    v.drain().then_inc(SM["v_cw"], 1)
                for m in range(NM):
                    GM = g5 * NM + m
                    if m >= 1:
                        v.wait_ge(SM["p_ms"], base + 10 + m)
                        v.tensor_copy(cbc[:], ps_ms[(m - 1) % 2][:])
                        v.drain().then_inc(SM["v_cb"], 1)
                    for i in range(NI):
                        GP = GM * NI + i
                        v.wait_ge(SM["a_sil"], GP + 1)
                        if i == 0 and GM >= 1:
                            v.wait_ge(SM["p_y"], GM * 8)
                        v.tensor_mul(a_t[:, i, :], sg[:, GP % 2, :],
                                     ps_gu[(GP % 2) * 2 + 1][:]).then_inc(SM["v_a"], 1)
                    for hc in range(8):
                        GY = GM * 8 + hc
                        v.wait_ge(SM["p_y"], GY + 1)
                        if m == 0:
                            v.tensor_add(x_sb[:, hc, GSL], x_sb[:, hc, GSL],
                                         ps_y[GY % 2][:]).then_inc(SM["v_y"], 1)
                        else:
                            v.tensor_mul(mean_b[:], ps_y[GY % 2][:], cbc[:])
                            v.drain()
                            v.tensor_add(x_sb[:, hc, GSL], x_sb[:, hc, GSL],
                                         mean_b[:]).then_inc(SM["v_y"], 1)

        @block.gpsimd
        def _(g):
            g.dma_start(out=x_sb[:], in_=x_in).then_inc(SM["s_x"], 16)
            g.dma_start(out=idb[:], in_=const_in[:, C_ID:C_ID + 128]).then_inc(SM["s_kb"], 16)
            g.dma_start(out=maskb[:], in_=const_in[:, C_MASK:C_MASK + 512]).then_inc(SM["s_kb"], 16)
            g.dma_start(out=onescol_b[:], in_=const_in[:, C_ONESCOL:C_ONESCOL + 1],
                        allow_slow_non_contiguous=True).then_inc(SM["s_kb"], 16)
            g.dma_start(out=selb[:], in_=const_in[0:8, C_SEL:C_SEL + 1024]).then_inc(SM["s_kb"], 16)
            g.dma_start(out=gimgb[:], in_=gate_in).then_inc(SM["s_kb"], 16)
            if PHASE_LIMIT < 3:
                return
            emit_precast_loads(g)

        @block.sync
        def _(sy):
            sy.dma_start(out=kf[:, 0:128], in_=const_in[:, 0:128]).then_inc(SM["s_k"], 16)
            sy.dma_start(out=kf[:, 128:260], in_=const_in[:, 256:388]).then_inc(SM["s_k"], 16)
            if PHASE_LIMIT >= 3:
                emit_precast_stores(sy)
            if PHASE_LIMIT < 3:
                if PHASE_LIMIT == 0:
                    sy.wait_ge(SM["s_x"], 16)
                    sy.wait_ge(SM["s_kb"], 80)
                elif PHASE_LIMIT == 1:
                    if P1_LEVEL == 1:
                        sy.wait_ge(SM["v1"], 64)
                    elif P1_LEVEL == 2:
                        sy.wait_ge(SM["pxs"], 128)
                    elif P1_LEVEL == 3:
                        sy.wait_ge(SM["v2"], 16)
                    else:
                        sy.wait_ge(SM["v3"], 8)
                else:
                    sy.wait_ge(SM["v_xup"], 128)
                nso = 0
                for g5 in range(NG5):
                    for c in range(8):
                        sy.dma_start(out=y_out[:, c, G5SL[g5]],
                                     in_=x_sb[:, c, G5SL[g5]]).then_inc(SM["s_out"], 16)
                        nso += 1
                sy.wait_ge(SM["s_out"], 16 * nso)
                return
            def pc_wait(chunk_idx):
                sy.wait_ge(SM["s_st0"], 16 * PC_CUM[chunk_idx])

            for g5 in range(NG5):
                def issue_wgu(m_, i):
                    GP = (g5 * NM + m_) * NI + i
                    if g5 == 0:
                        pc_wait(m_ * 38 + i)
                    if GP == 0:
                        sy.wait_ge(SM["v_xup"], 128)
                    if GP >= 3:
                        sy.wait_ge(SM["p_gu"], GP - 2)
                    sy.dma_start(out=wgu_buf[:, GP % 3, :],
                                 in_=wgu_bf[m_, :, i, :]).then_inc(
                                     SM[f"s_wg{GP % 3}"], 16)

                for m in range(NM):
                    GM = g5 * NM + m
                    if m == 0:
                        issue_wgu(m, 0)
                    for i in range(1, NI):
                        issue_wgu(m, i)
                    if m < NM - 1:
                        issue_wgu(m + 1, 0)
                    for hc in range(8):
                        for half in range(2):
                            WH = 2 * (GM * 8 + hc) + half
                            if g5 == 0:
                                pc_wait(m * 38 + 22 + 2 * hc + half)
                            if WH >= 4:
                                sy.wait_ge(SM["p_y"], (WH - 4) // 2 + 1)
                            sy.dma_start(out=wd_buf[:, WH % 4, :],
                                         in_=wd_bf[m, :, hc, half * 1408:(half + 1) * 1408]
                                         ).then_inc(SM[f"s_wd{WH % 4}"], 16)
                    if m == 0 and g5 >= 1:
                        pg = g5 - 1
                        sy.wait_ge(SM["v_y"], (pg * NM + NM) * 8)
                        for c in range(8):
                            sy.dma_start(out=y_out[:, c, G5SL[pg]],
                                         in_=x_sb[:, c, G5SL[pg]]).then_inc(SM["s_out"], 16)
            sy.wait_ge(SM["v_y"], (7 * NM + NM) * 8)
            for c in range(8):
                sy.dma_start(out=y_out[:, c, G5SL[7]],
                             in_=x_sb[:, c, G5SL[7]]).then_inc(SM["s_out"], 16)
            sy.wait_ge(SM["s_out"], 16 * 64)

    es.close()
    return nc


_BUILT = {}


def _get_nc():
    if "nc" not in _BUILT:
        _BUILT["nc"] = build_nc()
    return _BUILT["nc"]


def _host_images(L0_attn_x, L0_ffn_gate_w, L0_ffn_base_wgu, L0_ffn_base_wd,
                 L0_ffn_exp_wgu, L0_ffn_exp_wd):
    x = np.asarray(L0_attn_x, dtype=np.float32)
    gate_w = np.asarray(L0_ffn_gate_w, dtype=np.float32)
    base_wgu = np.asarray(L0_ffn_base_wgu, dtype=np.float32)
    base_wd = np.asarray(L0_ffn_base_wd, dtype=np.float32)
    exp_wgu = np.asarray(L0_ffn_exp_wgu, dtype=np.float32)
    exp_wd = np.asarray(L0_ffn_exp_wd, dtype=np.float32)

    x_imgs = []
    for b in range(NCORE):
        xi = np.ascontiguousarray(
            x[b].T.reshape(NCH, 128, S).transpose(1, 0, 2))
        x_imgs.append(xi)

    wgu_img = np.empty((NM, 128, NI, 2048), np.float32)
    wd_img = np.empty((NM, 128, 8, 2816), np.float32)
    for m in range(NM):
        wgu = base_wgu if m == 0 else exp_wgu[m - 1]
        wd = base_wd if m == 0 else exp_wd[m - 1]
        w4 = wgu.reshape(8, 128, 5632)
        gpart = w4[:, :, :2816].reshape(8, 128, NI, 128)
        upart = w4[:, :, 2816:].reshape(8, 128, NI, 128)
        cat = np.concatenate([gpart, upart], axis=3)
        wgu_img[m] = cat.transpose(1, 2, 0, 3).reshape(128, NI, 2048)
        wd4 = wd.reshape(NI, 128, 8, 128)
        wd_img[m] = wd4.transpose(1, 2, 0, 3).reshape(128, 8, 2816)

    gate_img = np.ascontiguousarray(
        gate_w.reshape(8, 128, 8).transpose(1, 0, 2).reshape(128, 64))

    cimg = np.zeros((128, C_W), np.float32)
    cimg[:, C_ID:C_ID + 128] = np.eye(128, dtype=np.float32)
    kk, jj = np.meshgrid(np.arange(128), np.arange(128), indexing="ij")
    cimg[:, C_MASK:C_MASK + 128] = np.where(kk > jj, -1000.0, 0.0)
    cimg[:, C_ONESCOL] = 1.0
    cimg[0, C_ONESROW:C_ONESROW + 128] = 1.0
    cimg[0, C_EPS] = 1e-5
    cimg[0, C_NEGHALF] = -0.5
    cimg[0, C_HALF] = 0.5
    sel = np.zeros((8, 1024), np.float32)
    for e in range(8):
        sel[e, e * 128:(e + 1) * 128] = 1.0
    cimg[0:8, C_SEL:C_SEL + 1024] = sel

    return x_imgs, wgu_img, wd_img, gate_img, cimg


def kernel(L0_attn_x, L0_ffn_gate_w, L0_ffn_base_wgu, L0_ffn_base_wd,
           L0_ffn_exp_wgu, L0_ffn_exp_wd):
    x_imgs, wgu_img, wd_img, gate_img, cimg = _host_images(
        L0_attn_x, L0_ffn_gate_w, L0_ffn_base_wgu, L0_ffn_base_wd,
        L0_ffn_exp_wgu, L0_ffn_exp_wd)
    nc = _get_nc()
    in_maps = [{
        "x_img": x_imgs[b],
        "wgu_img": wgu_img,
        "wd_img": wd_img,
        "gate_img": gate_img,
        "const_img": cimg,
    } for b in range(NCORE)]
    res = run_bass_kernel_spmd(nc, in_maps, core_ids=list(range(NCORE)))
    out = np.empty((NCORE, S, D), np.float32)
    for b in range(NCORE):
        yi = np.asarray(res.results[b]["y_img"], np.float32)
        out[b] = yi.transpose(1, 0, 2).reshape(D, S).T
    return out


# version 19
# speedup vs baseline: 1.1250x; 1.0841x over previous
import numpy as np
import ml_dtypes
from contextlib import ExitStack

import concourse.bass as bass
import concourse.mybir as mybir
from concourse.bass_utils import run_bass_kernel_spmd

F32 = mybir.dt.float32
BF16 = mybir.dt.bfloat16
AF = mybir.ActivationFunctionType
ALU = mybir.AluOpType
AX = mybir.AxisListType

NCORE = 8
S = 4096
D = 1024
NCH = 8
NH = 16
NQ = 8
NI = 22
NM = 9
NG5 = 8
PHASE_LIMIT = 3
P1_LEVEL = 4

def kbs_of(Q):
    return 4 * (Q + 1)

TPH = sum(kbs_of(Q) for Q in range(NQ))
QB = [0]
for _q in range(NQ):
    QB.append(QB[-1] + kbs_of(_q))

def gti(h, Q, kb):
    return h * TPH + QB[Q] + kb

NTILE = NH * TPH

C_ID = 0
C_TRI = 128
C_ONESCOL = 256
C_ONESROW = 257
C_EPS = 385
C_NEGHALF = 386
C_HALF = 387
C_SEL = 392
C_MASK = C_SEL + 1024
C_W = C_MASK + 512


def build_nc():
    nc = bass.Bass("TRN2", target_bir_lowering=False)

    x_in = nc.dram_tensor("x_img", [128, NCH, S], F32, kind="ExternalInput").ap()
    y_out = nc.dram_tensor("y_img", [128, NCH, S], F32, kind="ExternalOutput").ap()
    wgu_in = nc.dram_tensor("wgu_img", [NM, 128, NI, 2048], F32, kind="ExternalInput").ap()
    wd_in = nc.dram_tensor("wd_img", [NM, 128, 8, 2816], F32, kind="ExternalInput").ap()
    gate_in = nc.dram_tensor("gate_img", [128, 64], F32, kind="ExternalInput").ap()
    const_in = nc.dram_tensor("const_img", [128, C_W], F32, kind="ExternalInput").ap()
    wgu_bf = nc.dram_tensor("wgu_bf", [NM, 128, NI, 2048], BF16, kind="Internal").ap()
    wd_bf = nc.dram_tensor("wd_bf", [NM, 128, 8, 2816], BF16, kind="Internal").ap()

    es = ExitStack()
    sb = es.enter_context

    kf = sb(nc.sbuf_tensor("kf", [128, 260], F32))
    idb = sb(nc.sbuf_tensor("idb", [128, 128], BF16))
    maskb = sb(nc.sbuf_tensor("maskb", [128, 512], BF16))
    onescol_b = sb(nc.sbuf_tensor("onescol_b", [128, 1], BF16))
    selb = sb(nc.sbuf_tensor("selb", [8, 1024], BF16))
    gimgb = sb(nc.sbuf_tensor("gimgb", [128, 64], BF16))
    x_sb = sb(nc.sbuf_tensor("x_sb", [128, NCH, S], F32))
    bounce = sb(nc.sbuf_tensor("bounce", [128, 3, 512], BF16))
    xb = sb(nc.sbuf_tensor("xb", [128, 2, 512], BF16))
    xsq = sb(nc.sbuf_tensor("xsq", [128, 2, 512], BF16))
    RA = sb(nc.sbuf_tensor("RA", [33, 512], F32))
    RB = sb(nc.sbuf_tensor("RB", [33, 512], F32))
    r0 = RA[0:1, :]
    r1 = RA[32:33, :]
    r3v = RB[0:1, :]
    r2 = RB[32:33, :]
    mr_col = sb(nc.sbuf_tensor("mr_col", [128, 32], F32))
    r_col = sb(nc.sbuf_tensor("r_col", [128, 32], F32))

    id_f = kf[:, 0:128]
    id1 = kf[0:1, 0:1]
    ones1x128f = kf[0:1, 129:257]
    ones1x64f = kf[0:1, 129:193]
    eps_cell = kf[0:1, 257:258]
    neghalf_cell = kf[0:1, 258:259]
    half_cell = kf[0:1, 259:260]

    esB = ExitStack()
    r4 = esB.enter_context(nc.sbuf_tensor("r4", [1, 512], F32))
    h_tm = [esB.enter_context(nc.sbuf_tensor(f"h_tm{i}", [128, 32, 130], BF16)) for i in range(2)]
    h65 = [esB.enter_context(nc.sbuf_tensor(f"h65_{i}", [65, S], BF16)) for i in range(2)]
    aug = [esB.enter_context(nc.sbuf_tensor(f"aug{i}", [65, 512], BF16)) for i in range(4)]
    n2row = [esB.enter_context(nc.sbuf_tensor(f"n2row{i}", [1, 512], BF16)) for i in range(2)]
    sqt = [esB.enter_context(nc.sbuf_tensor(f"sqt{i}", [64, 512], BF16)) for i in range(2)]
    pt = [esB.enter_context(nc.sbuf_tensor(f"pt{i}", [128, 512], BF16)) for i in range(5)]
    o_sb = [esB.enter_context(nc.sbuf_tensor(f"o_sb{i}", [65, 512], F32)) for i in range(2)]
    t1t = esB.enter_context(nc.sbuf_tensor("t1t", [128, 512], F32))
    rlr = [esB.enter_context(nc.sbuf_tensor(f"rlr{i}", [1, 512], F32)) for i in range(2)]
    cg = esB.enter_context(nc.sbuf_tensor("cg", [1, 8], F32))
    esB.close()

    esC = ExitStack()
    a_t = esC.enter_context(nc.sbuf_tensor("a_t", [128, NI, 512], BF16))
    h2s = esC.enter_context(nc.sbuf_tensor("h2s", [128, NCH, 512], BF16))
    wgu_buf = esC.enter_context(nc.sbuf_tensor("wgu_buf", [128, 3, 2048], BF16))
    wd_buf = esC.enter_context(nc.sbuf_tensor("wd_buf", [128, 4, 1408], BF16))
    sg = esC.enter_context(nc.sbuf_tensor("sg", [128, 2, 512], F32))
    cwt = esC.enter_context(nc.sbuf_tensor("cwt", [8, 512], BF16))
    cbc = esC.enter_context(nc.sbuf_tensor("cbc", [128, 512], F32))
    mean_b = esC.enter_context(nc.sbuf_tensor("mean_b", [128, 512], F32))
    rstd_b = esC.enter_context(nc.sbuf_tensor("rstd_b", [128, 512], F32))
    lg_sb = esC.enter_context(nc.sbuf_tensor("lg_sb", [128, 8], F32))
    e_g = esC.enter_context(nc.sbuf_tensor("e_g", [128, 8], F32))
    k_g = esC.enter_context(nc.sbuf_tensor("k_g", [128, 8], F32))
    cw_g = esC.enter_context(nc.sbuf_tensor("cw_g", [128, 8], F32))
    gcl = esC.enter_context(nc.sbuf_tensor("gcl", [128, 8], F32))
    esC.close()

    esP1 = ExitStack()
    ps_m1 = esP1.enter_context(nc.psum_tensor("ps_m1", [1, 512], F32))
    ps_m2 = esP1.enter_context(nc.psum_tensor("ps_m2", [1, 512], F32))
    ps_tc = esP1.enter_context(nc.psum_tensor("ps_tc", [128, 8], F32))
    esP1.close()
    esP2 = ExitStack()
    ps_s = [esP2.enter_context(nc.psum_tensor(f"ps_s{i}", [128, 512], F32)) for i in range(3)]
    ps_o = [esP2.enter_context(nc.psum_tensor(f"ps_o{i}", [65, 512], F32)) for i in range(2)]
    ps_bx = esP2.enter_context(nc.psum_tensor("ps_bx", [128, 512], F32))
    ps_bh = esP2.enter_context(nc.psum_tensor("ps_bh", [64, 512], BF16))
    ps_nrl = esP2.enter_context(nc.psum_tensor("ps_nrl", [65, 512], F32))
    esP2.close()
    esP3 = ExitStack()
    ps_gu = [esP3.enter_context(nc.psum_tensor(f"ps_gu{i}", [128, 512], F32)) for i in range(4)]
    ps_y = [esP3.enter_context(nc.psum_tensor(f"ps_y{i}", [128, 512], F32)) for i in range(2)]
    ps_ms = [esP3.enter_context(nc.psum_tensor(f"ps_ms{i}", [128, 512], F32)) for i in range(2)]
    esP3.close()

    sem_names = [
        "s_x", "s_k", "s_kb", "s_ld0", "s_st0",
        "s_wg0", "s_wg1", "s_wg2",
        "s_wd0", "s_wd1", "s_wd2", "s_wd3", "s_out",
        "a1", "v1", "pxs", "a2", "v2", "p1t", "v3",
        "a_sq", "p_n2", "v_aug", "p_s", "a_exp", "p_pv",
        "v_rl", "p_rlb", "v_xup", "p_bx", "v_bx", "p_bh", "v_bh",
        "a13", "v13", "pxs3", "a23", "v23", "v_st", "p_ms", "v_gt1",
        "a_g", "v_gt2", "v_cw", "p_gu", "a_sil", "v_a", "p_y", "v_y",
        "v_cb", "v_h2",
    ]
    SM = {n: sb(nc.semaphore(n)) for n in sem_names}

    def mean_cnt(ci):
        return 2 * ci + 1

    def sq_cnt(ci):
        return 2 * ci + 2

    def head_builds(h):
        out = []
        if h >= 15:
            return out
        if h % 2 == 1:
            p = (h + 1) // 2
            out += [("bx", p, g) for g in range(8)]
        out += [("bh", h + 1, g) for g in range(8)]
        return out

    G5SL = [slice(g5 * 512, (g5 + 1) * 512) for g5 in range(NG5)]

    PC_PIECES = []
    PC_CUM = []
    _cum = 0
    for _m in range(NM):
        for _i in range(NI):
            for _qq in range(4):
                PC_PIECES.append(("wgu", _m, _i, 0, _qq * 512, 512))
            _cum += 4
            PC_CUM.append(_cum)
        for _hc in range(8):
            for _half in range(2):
                for _off, _w in ((0, 512), (512, 512), (1024, 384)):
                    PC_PIECES.append(("wd", _m, _hc, _half, _off, _w))
                _cum += 3
                PC_CUM.append(_cum)

    def _pc_aps(piece):
        kind, m_, a, b, off, w = piece
        if kind == "wgu":
            return (wgu_in[m_, :, a, off:off + w],
                    wgu_bf[m_, :, a, off:off + w], w)
        cs = b * 1408 + off
        return (wd_in[m_, :, a, cs:cs + w],
                wd_bf[m_, :, a, cs:cs + w], w)

    def emit_precast_loads(eng):
        for p, piece in enumerate(PC_PIECES):
            src_ap, _, w = _pc_aps(piece)
            if p >= 3:
                eng.wait_ge(SM["s_st0"], 16 * (p - 2))
            if PHASE_LIMIT >= 2 and 2 * p - 300 > 0:
                eng.wait_ge(SM["a_exp"], min(2 * p - 300, NTILE))
            eng.dma_start(out=bounce[:, p % 3, 0:w],
                          in_=src_ap).then_inc(SM["s_ld0"], 16)

    def emit_precast_stores(eng):
        for p, piece in enumerate(PC_PIECES):
            _, dst_ap, w = _pc_aps(piece)
            eng.wait_ge(SM["s_ld0"], 16 * (p + 1))
            eng.dma_start(out=dst_ap,
                          in_=bounce[:, p % 3, 0:w]).then_inc(SM["s_st0"], 16)

    with nc.Block() as block:

        @block.tensor
        def _(t):
            if PHASE_LIMIT < 1 or P1_LEVEL < 2:
                if PHASE_LIMIT >= 1 and P1_LEVEL >= 2:
                    pass
                if PHASE_LIMIT < 1 or P1_LEVEL < 2:
                    return
            for tt in range(8):
                for c in range(8):
                    ci = 8 * tt + c
                    t.wait_ge(SM["a1"], ci + 1)
                    if ci == 0:
                        t.wait_ge(SM["s_kb"], 16 * 5)
                        t.wait_ge(SM["s_k"], 32)
                    if tt > 0 and c == 0 and P1_LEVEL >= 3:
                        t.wait_ge(SM["a2"], 2 * (tt - 1) + 1)
                    t.matmul(ps_m1[:], onescol_b[:], xb[:, c % 2, :],
                             start=(c == 0), stop=(c == 7),
                             skip_group_check=True).then_inc(SM["pxs"], 1)
                    t.wait_ge(SM["v1"], ci + 1)
                    t.matmul(ps_m2[:], onescol_b[:], xsq[:, c % 2, :],
                             start=(c == 0), stop=(c == 7),
                             skip_group_check=True).then_inc(SM["pxs"], 1)
                if P1_LEVEL < 4:
                    continue
                t.wait_ge(SM["v2"], 2 * tt + 2)
                if tt > 0:
                    t.wait_ge(SM["v3"], tt)
                for j in range(4):
                    t.transpose(ps_tc[:, j:j + 1], r4[0:1, j * 128:(j + 1) * 128], id1)
                for j in range(4):
                    mmtc = t.transpose(ps_tc[:, 4 + j:5 + j], r3v[0:1, j * 128:(j + 1) * 128], id1)
                mmtc.then_inc(SM["p1t"], 1)

            if PHASE_LIMIT < 2:
                return
            def pe_build(kind, pidx, g):
                if kind == "bx":
                    if 8 * pidx + g >= 1:
                        t.wait_ge(SM["v_bx"], 8 * pidx + g)
                    for j in range(4):
                        tc = g * 4 + j
                        mm = t.transpose(ps_bx[:, j * 128:(j + 1) * 128],
                                         x_sb[:, pidx, tc * 128:(tc + 1) * 128], id_f)
                    mm.then_inc(SM["p_bx"], 1)
                else:
                    hb = pidx
                    t.wait_ge(SM["v_bh"], 8 * hb + g)
                    t.wait_ge(SM["v_bx"], 8 * (hb // 2) + g + 1)
                    for j in range(4):
                        tc = g * 4 + j
                        mm = t.transpose(ps_bh[0:64, j * 128:(j + 1) * 128],
                                         h_tm[(hb // 2) % 2][:, tc, 65 * (hb % 2):65 * (hb % 2) + 64],
                                         idb[:])
                    mm.then_inc(SM["p_bh"], 1)

            for g in range(8):
                pe_build("bx", 0, g)
            for g in range(8):
                pe_build("bh", 0, g)

            for h in range(NH):
                pair = h // 2
                builds = head_builds(h)
                bpos = 0
                def pe_n2(Qn):
                    AQn = 8 * h + Qn
                    t.wait_ge(SM["a_sq"], AQn + 1)
                    if AQn >= 1:
                        t.wait_ge(SM["v_aug"], AQn)
                    t.matmul(ps_nrl[64:65, :], onescol_b[0:64, :], sqt[Qn % 2][:],
                             start=True, stop=True,
                             skip_group_check=True).then_inc(SM["p_n2"], 1)

                def emit_pv(Q, kb):
                    T = gti(h, Q, kb)
                    AQ = 8 * h + Q
                    nkb = kbs_of(Q)
                    sk = max(128 * kb - 512 * Q, 0)
                    t.wait_ge(SM["a_exp"], T + 1)
                    if kb == 0:
                        if AQ >= 2:
                            t.wait_ge(SM["v_rl"], AQ - 1)
                        if Q == 0:
                            t.wait_ge(SM["v_bx"], 8 * (pair + 1))
                    t.matmul(ps_o[AQ % 2][0:65, sk:512],
                             h_tm[pair % 2][:, kb, 65 * (h % 2):65 * (h % 2) + 65],
                             pt[T % 5][:, sk:512],
                             start=(kb == 0), stop=(kb == nkb - 1),
                             skip_group_check=True).then_inc(SM["p_pv"], 1)

                pe_n2(0)
                pe_n2(1)
                pend = []
                defer = []
                for Q in range(NQ):
                    AQ = 8 * h + Q
                    nkb = kbs_of(Q)
                    rlb_slot = 3 if Q == 0 else 5
                    for kb in range(nkb):
                        T = gti(h, Q, kb)
                        delta = 128 * kb - 512 * Q
                        sk = max(delta, 0)

                        if kb == rlb_slot and AQ >= 1:
                            pv = AQ - 1
                            t.wait_ge(SM["v_rl"], pv + 1)
                            if pv >= 1:
                                t.wait_ge(SM["v_xup"], pv)
                            t.matmul(ps_nrl[0:64, :], ones1x64f, rlr[pv % 2][:],
                                     start=True, stop=True,
                                     skip_group_check=True).then_inc(SM["p_rlb"], 1)
                        if kb == 0 and Q == 0:
                            t.wait_ge(SM["v_bh"], 8 * (h + 1))
                        if kb == 0:
                            t.wait_ge(SM["v_aug"], AQ + 1)
                        if T >= 3:
                            t.wait_ge(SM["a_exp"], T - 2)
                        diag = delta >= 0
                        mm = t.matmul(ps_s[T % 3][:, sk:512],
                                      h65[h % 2][0:65, kb * 128:(kb + 1) * 128],
                                      aug[Q % 4][0:65, sk:512],
                                      start=True, stop=(not diag),
                                      skip_group_check=True)
                        if diag:
                            mm = t.matmul(ps_s[T % 3][:, sk:512],
                                          idb[:], maskb[:, 0:512 - sk],
                                          start=False, stop=True,
                                          skip_group_check=True)
                        mm.then_inc(SM["p_s"], 1)
                        pend.append((Q, kb))
                        if len(pend) > 4:
                            emit_pv(*pend.pop(0))
                        if kb == 3:
                            for job in defer:
                                job()
                            defer = []
                    if Q + 2 <= 7:
                        defer.append(lambda Qn=Q + 2: pe_n2(Qn))
                    for _ in range(2 if len(builds) > 8 else 1):
                        if bpos < len(builds):
                            b = builds[bpos]; bpos += 1
                            defer.append(lambda b=b: pe_build(*b))
                while pend:
                    emit_pv(*pend.pop(0))
                for job in defer:
                    job()
                defer = []
                while bpos < len(builds):
                    pe_build(*builds[bpos]); bpos += 1
            t.wait_ge(SM["v_rl"], 128)
            t.wait_ge(SM["v_xup"], 127)
            t.matmul(ps_nrl[0:64, :], ones1x64f, rlr[127 % 2][:],
                     start=True, stop=True,
                     skip_group_check=True).then_inc(SM["p_rlb"], 1)

            if PHASE_LIMIT < 3:
                return
            for g5 in range(NG5):
                base = 18 * g5
                for c in range(8):
                    ci = 8 * g5 + c
                    t.wait_ge(SM["a13"], ci + 1)
                    if c == 0 and g5 >= 1:
                        t.wait_ge(SM["v_cb"], 8 * (g5 - 1) + 7)
                        t.wait_ge(SM["v_cb"], 8 * (g5 - 1) + 8)
                    t.matmul(ps_ms[0][0:1, :], onescol_b[:], xb[:, c % 2, :],
                             start=(c == 0), stop=(c == 7),
                             skip_group_check=True).then_inc(SM["pxs3"], 1)
                    t.wait_ge(SM["v13"], ci + 1)
                    t.matmul(ps_ms[1][0:1, :], onescol_b[:], xsq[:, c % 2, :],
                             start=(c == 0), stop=(c == 7),
                             skip_group_check=True).then_inc(SM["pxs3"], 1)
                t.wait_ge(SM["v23"], 2 * g5 + 2)
                t.wait_ge(SM["a23"], 2 * g5 + 1)
                t.matmul(ps_ms[0][:], ones1x128f, r0,
                         start=True, stop=True).then_inc(SM["p_ms"], 1)
                t.matmul(ps_ms[1][:], ones1x128f, r3v,
                         start=True, stop=True).then_inc(SM["p_ms"], 1)
                def pe_lg(t1):
                    bank = t1 % 2
                    if t1 == 0:
                        t.wait_ge(SM["v_st"], 2 * g5 + 1)
                        t.wait_ge(SM["v_h2"], g5 + 1)
                    elif t1 == 1:
                        t.wait_ge(SM["v_st"], 2 * g5 + 2)
                    else:
                        t.wait_ge(SM["v_cw"], 4 * g5 + t1 - 1)
                    for kc in range(8):
                        mm = t.matmul(ps_ms[bank][:, 0:8],
                                      h2s[:, kc, t1 * 128:(t1 + 1) * 128],
                                      gimgb[:, kc * 8:(kc + 1) * 8],
                                      start=(kc == 0), stop=(kc == 7))
                    mm.then_inc(SM["p_ms"], 1)

                def pe_cwt(t1):
                    bank = t1 % 2
                    t.wait_ge(SM["v_gt1"], 4 * g5 + t1 + 1)
                    t.wait_ge(SM["v_gt2"], 4 * g5 + t1 + 1)
                    t.transpose(ps_ms[bank][0:8, 0:128], cw_g[:, 0:8],
                                id_f).then_inc(SM["p_ms"], 1)

                pe_lg(0)
                pe_lg(1)
                pe_cwt(0)
                pe_lg(2)
                pe_cwt(1)
                pe_lg(3)
                pe_cwt(2)
                pe_cwt(3)
                def emit_gu(m_, i):
                    GP = (g5 * NM + m_) * NI + i
                    t.wait_ge(SM[f"s_wg{GP % 3}"], 16 * (GP // 3 + 1))
                    if GP >= 2:
                        t.wait_ge(SM["v_a"], GP - 1)
                    if i == 0:
                        t.wait_ge(SM["v_h2"], g5 + 1)
                    bg = (GP % 2) * 2
                    bu = bg + 1
                    for k in range(8):
                        t.matmul(ps_gu[bg][:], wgu_buf[:, GP % 3, k * 256:k * 256 + 128],
                                 h2s[:, k, :], start=(k == 0), stop=(k == 7))
                    for k in range(8):
                        mm = t.matmul(ps_gu[bu][:], wgu_buf[:, GP % 3, k * 256 + 128:k * 256 + 256],
                                      h2s[:, k, :], start=(k == 0), stop=(k == 7))
                    mm.then_inc(SM["p_gu"], 1)

                for m in range(NM):
                    GM = g5 * NM + m
                    if m >= 1:
                        bank = (m - 1) % 2
                        if m <= 2:
                            t.wait_ge(SM["v_cw"], 4 * g5 + 2 + m)
                        else:
                            t.wait_ge(SM["v_cb"], 8 * g5 + m - 2)
                        t.matmul(ps_ms[bank][:], selb[:, (m - 1) * 128:m * 128],
                                 cwt[0:8, :], start=True, stop=True).then_inc(SM["p_ms"], 1)
                    for i in (range(NI) if m == 0 else range(1, NI)):
                        emit_gu(m, i)
                    if m < NM - 1:
                        emit_gu(m + 1, 0)
                    for hc in range(8):
                        GY = GM * 8 + hc
                        if hc == 0:
                            t.wait_ge(SM["v_a"], GM * NI + NI)
                        if GY >= 2:
                            t.wait_ge(SM["v_y"], GY - 1)
                        for half in range(2):
                            WH = 2 * GY + half
                            t.wait_ge(SM[f"s_wd{WH % 4}"], 16 * (WH // 4 + 1))
                            for il in range(11):
                                ic = half * 11 + il
                                mm = t.matmul(ps_y[GY % 2][:],
                                              wd_buf[:, WH % 4, il * 128:(il + 1) * 128],
                                              a_t[:, ic, :],
                                              start=(ic == 0), stop=(ic == NI - 1))
                        mm.then_inc(SM["p_y"], 1)

        @block.scalar
        def _(s):
            if PHASE_LIMIT < 1:
                return
            for tt in range(8):
                for c in range(8):
                    ci = 8 * tt + c
                    if ci == 0:
                        s.wait_ge(SM["s_x"], 16)
                    if ci >= 2:
                        s.wait_ge(SM["v1"], ci - 1)
                        if P1_LEVEL >= 2:
                            s.wait_ge(SM["pxs"], mean_cnt(ci - 2))
                    s.copy(xb[:, c % 2, :],
                           x_sb[:, c, tt * 512:(tt + 1) * 512]).then_inc(SM["a1"], 1)
                if P1_LEVEL < 3:
                    continue
                s.wait_ge(SM["pxs"], 16 * tt + 16)
                if tt >= 1:
                    s.wait_ge(SM["v2"], 2 * (tt - 1) + 2)
                s.mul(r0, ps_m1[0:1, :], 1.0 / 1024.0)
                s.mul(r1, ps_m2[0:1, :], 1.0 / 1024.0)
                s.drain().then_inc(SM["a2"], 1)
                if ci == 7 and tt == 0:
                    s.wait_ge(SM["s_k"], 32)
                s.wait_ge(SM["v2"], 2 * tt + 1)
                if P1_LEVEL >= 4:
                    s.wait_ge(SM["p1t"], tt)
                s.activation(r2, r1, AF.Sqrt, bias=eps_cell)
                s.drain().then_inc(SM["a2"], 1)

            if PHASE_LIMIT < 2:
                return
            for h in range(NH):
                for Q in range(NQ):
                    nkb = kbs_of(Q)
                    for kb in range(nkb):
                        g = gti(h, Q, kb)
                        sk = max(128 * kb - 512 * Q, 0)
                        s.wait_ge(SM["p_s"], g + 1)
                        if g >= 5:
                            s.wait_ge(SM["p_pv"], g - 4)
                        s.activation(pt[g % 5][:, sk:512], ps_s[g % 3][:, sk:512],
                                     AF.Exp, scale=0.125).then_inc(SM["a_exp"], 1)

            if PHASE_LIMIT < 3:
                return
            for g5 in range(NG5):
                base = 18 * g5
                GSL = G5SL[g5]
                for c in range(8):
                    ci = 8 * g5 + c
                    if ci == 0:
                        s.wait_ge(SM["v_xup"], 128)
                    if ci >= 2:
                        s.wait_ge(SM["v13"], ci - 1)
                        s.wait_ge(SM["pxs3"], mean_cnt(ci - 2))
                    s.copy(xb[:, c % 2, :], x_sb[:, c, GSL]).then_inc(SM["a13"], 1)
                s.wait_ge(SM["pxs3"], 16 * g5 + 16)
                if g5 >= 1:
                    s.wait_ge(SM["v23"], 2 * (g5 - 1) + 2)
                    s.wait_ge(SM["p_ms"], 18 * (g5 - 1) + 2)
                s.mul(r0, ps_ms[0][0:1, :], 1.0 / 1024.0)
                s.mul(r1, ps_ms[1][0:1, :], 1.0 / 1024.0)
                s.drain().then_inc(SM["a23"], 1)
                s.wait_ge(SM["v23"], 2 * g5 + 1)
                s.activation(r2, r1, AF.Sqrt, bias=eps_cell)
                s.drain().then_inc(SM["a23"], 1)
                for t1 in range(4):
                    s.wait_ge(SM["v_gt1"], 4 * g5 + t1 + 1)
                    s.activation(e_g[:], lg_sb[:], AF.Exp, bias=gcl[:, 0:1])
                    s.drain().then_inc(SM["a_g"], 1)
                for m in range(NM):
                    GM = g5 * NM + m
                    for i in range(NI):
                        GP = GM * NI + i
                        s.wait_ge(SM["p_gu"], GP + 1)
                        if GP >= 2:
                            s.wait_ge(SM["v_a"], GP - 1)
                        s.activation(sg[:, GP % 2, :], ps_gu[(GP % 2) * 2][:],
                                     AF.Silu).then_inc(SM["a_sil"], 1)

        @block.vector
        def _(v):
            if PHASE_LIMIT < 1:
                return
            for tt in range(8):
                for c in range(8):
                    ci = 8 * tt + c
                    v.wait_ge(SM["a1"], ci + 1)
                    if ci >= 2 and P1_LEVEL >= 2:
                        v.wait_ge(SM["pxs"], sq_cnt(ci - 2))
                    v.tensor_mul(xsq[:, c % 2, :], xb[:, c % 2, :],
                                 xb[:, c % 2, :]).then_inc(SM["v1"], 1)
                if P1_LEVEL < 3:
                    continue
                v.wait_ge(SM["a2"], 2 * tt + 1)
                v.tensor_mul(r2, r0, r0)
                v.drain()
                v.tensor_sub(r1, r1, r2)
                v.drain().then_inc(SM["v2"], 1)
                v.wait_ge(SM["a2"], 2 * tt + 2)
                if tt >= 1 and P1_LEVEL >= 4:
                    v.wait_ge(SM["p1t"], tt)
                v.reciprocal(r3v, r2)
                v.drain()
                v.tensor_mul(r4[0:1, :], r0, r3v)
                v.drain().then_inc(SM["v2"], 1)
                if P1_LEVEL < 4:
                    continue
                v.wait_ge(SM["p1t"], tt + 1)
                v.tensor_copy(mr_col[:, 4 * tt:4 * tt + 4], ps_tc[:, 0:4])
                v.tensor_copy(r_col[:, 4 * tt:4 * tt + 4], ps_tc[:, 4:8])
                v.drain().then_inc(SM["v3"], 1)

            if PHASE_LIMIT < 2:
                return
            v.wait_ge(SM["s_k"], 32)
            v.memset(h65[0][64:65, :], 1.0)
            v.memset(h65[1][64:65, :], 1.0)
            v.drain()

            def dve_build(kind, pidx, g):
                if kind == "bx":
                    v.wait_ge(SM["p_bx"], 8 * pidx + g + 1)
                    v.wait_ge(SM["v3"], g + 1)
                    if g == 0:
                        if pidx >= 2:
                            v.wait_ge(SM["p_pv"], TPH * (2 * pidx - 2))
                        v.memset(h_tm[pidx % 2][:, :, 64:65], 1.0)
                        v.memset(h_tm[pidx % 2][:, :, 129:130], 1.0)
                    for j in range(4):
                        tc = g * 4 + j
                        v.tensor_scalar(h_tm[pidx % 2][:, tc, 0:64],
                                        ps_bx[:, j * 128:j * 128 + 64],
                                        r_col[:, tc:tc + 1], mr_col[:, tc:tc + 1],
                                        ALU.mult, ALU.subtract)
                        v.tensor_scalar(h_tm[pidx % 2][:, tc, 65:129],
                                        ps_bx[:, j * 128 + 64:(j + 1) * 128],
                                        r_col[:, tc:tc + 1], mr_col[:, tc:tc + 1],
                                        ALU.mult, ALU.subtract)
                    v.drain().then_inc(SM["v_bx"], 1)
                else:
                    hb = pidx
                    v.wait_ge(SM["p_bh"], 8 * hb + g + 1)
                    if g == 0 and hb >= 2:
                        v.wait_ge(SM["p_s"], TPH * (hb - 1))
                        v.wait_ge(SM["a_sq"], 8 * (hb - 1))
                    for j in range(4):
                        tc = g * 4 + j
                        v.tensor_copy(h65[hb % 2][0:64, tc * 128:(tc + 1) * 128],
                                      ps_bh[0:64, j * 128:(j + 1) * 128])
                    v.drain().then_inc(SM["v_bh"], 1)

            def dve_sq(h, Qn):
                AQn = 8 * h + Qn
                if Qn == 0:
                    v.wait_ge(SM["v_bh"], 8 * (h + 1))
                if AQn >= 2:
                    v.wait_ge(SM["p_n2"], AQn - 1)
                v.tensor_mul(sqt[Qn % 2][:],
                             h65[h % 2][0:64, Qn * 512:(Qn + 1) * 512],
                             h65[h % 2][0:64, Qn * 512:(Qn + 1) * 512]
                             ).then_inc(SM["a_sq"], 1)

            def dve_pre(h, Q):
                AQ = 8 * h + Q
                v.wait_ge(SM["p_n2"], AQ + 1)
                if Q == 0:
                    v.wait_ge(SM["p_s"], TPH * h)
                v.tensor_copy(n2row[Q % 2][0:1, :], ps_nrl[64:65, :])
                v.drain()
                v.reduce_max(cg[:, 0:1], n2row[Q % 2][0:1, :], AX.X, negate=True)
                v.drain()
                if Q == 0:
                    v.tensor_copy(cg[:, 1 + (Q % 2):2 + (Q % 2)], cg[:, 0:1])
                else:
                    v.tensor_tensor(cg[:, 1 + (Q % 2):2 + (Q % 2)],
                                    cg[:, 1 + ((Q - 1) % 2):2 + ((Q - 1) % 2)],
                                    cg[:, 0:1], ALU.min)
                v.drain()
                v.tensor_scalar_mul(cg[:, 3:4], cg[:, 1 + (Q % 2):2 + (Q % 2)], half_cell)
                v.drain()
                v.tensor_scalar(aug[Q % 4][64:65, :], n2row[Q % 2][0:1, :],
                                neghalf_cell, cg[:, 3:4], ALU.mult, ALU.add)
                v.tensor_copy(aug[Q % 4][0:64, :], h65[h % 2][0:64, Q * 512:(Q + 1) * 512])
                v.drain().then_inc(SM["v_aug"], 1)

            def dve_finB(aq):
                hh, QQ = aq // 8, aq % 8
                rowbase = 64 * (hh % 2)
                c = hh // 2
                v.wait_ge(SM["p_rlb"], aq + 1)
                v.tensor_mul(t1t[rowbase:rowbase + 64, :], o_sb[aq % 2][0:64, :],
                             ps_nrl[0:64, :])
                v.drain()
                v.tensor_add(x_sb[rowbase:rowbase + 64, c, QQ * 512:(QQ + 1) * 512],
                             x_sb[rowbase:rowbase + 64, c, QQ * 512:(QQ + 1) * 512],
                             t1t[rowbase:rowbase + 64, :])
                v.drain().then_inc(SM["v_xup"], 1)

            for g in range(8):
                dve_build("bx", 0, g)
            for g in range(8):
                dve_build("bh", 0, g)

            for h in range(NH):
                builds = head_builds(h)
                bpos = 0
                dve_sq(h, 0)
                dve_pre(h, 0)
                dve_sq(h, 1)
                dve_pre(h, 1)
                for Q in range(NQ):
                    AQ = 8 * h + Q
                    v.wait_ge(SM["p_pv"], h * TPH + QB[Q + 1])
                    v.tensor_copy(o_sb[AQ % 2][:], ps_o[AQ % 2][0:65, :])
                    v.drain()
                    v.reciprocal(rlr[AQ % 2][0:1, :], o_sb[AQ % 2][64:65, :])
                    v.drain().then_inc(SM["v_rl"], 1)
                    if AQ >= 1:
                        dve_finB(AQ - 1)
                    if Q + 2 <= 7:
                        dve_sq(h, Q + 2)
                    if Q + 2 <= 7:
                        dve_pre(h, Q + 2)
                    if bpos < len(builds):
                        dve_build(*builds[bpos]); bpos += 1
                    if bpos < len(builds) and len(builds) > 8:
                        dve_build(*builds[bpos]); bpos += 1
                while bpos < len(builds):
                    dve_build(*builds[bpos]); bpos += 1
            dve_finB(127)

            if PHASE_LIMIT < 3:
                return
            for g5 in range(NG5):
                base = 18 * g5
                GSL = G5SL[g5]
                for c in range(8):
                    ci = 8 * g5 + c
                    v.wait_ge(SM["a13"], ci + 1)
                    if ci >= 2:
                        v.wait_ge(SM["pxs3"], sq_cnt(ci - 2))
                    v.tensor_mul(xsq[:, c % 2, :], xb[:, c % 2, :],
                                 xb[:, c % 2, :]).then_inc(SM["v13"], 1)
                v.wait_ge(SM["a23"], 2 * g5 + 1)
                v.tensor_mul(r2, r0, r0)
                v.drain()
                v.tensor_sub(r1, r1, r2)
                v.drain().then_inc(SM["v23"], 1)
                v.wait_ge(SM["a23"], 2 * g5 + 2)
                if g5 >= 1:
                    v.wait_ge(SM["p_ms"], 18 * (g5 - 1) + 2)
                v.reciprocal(r3v, r2)
                v.drain().then_inc(SM["v23"], 1)
                v.wait_ge(SM["p_ms"], base + 1)
                v.tensor_copy(mean_b[:], ps_ms[0][:])
                v.drain().then_inc(SM["v_st"], 1)
                v.wait_ge(SM["p_ms"], base + 2)
                v.tensor_copy(rstd_b[:], ps_ms[1][:])
                v.drain().then_inc(SM["v_st"], 1)
                if g5 >= 1:
                    v.wait_ge(SM["p_gu"], (g5 * NM) * NI)
                    v.wait_ge(SM["p_ms"], 18 * (g5 - 1) + 8)
                for c in range(8):
                    v.tensor_sub(sg[:, 0, :], x_sb[:, c, GSL], mean_b[:])
                    v.drain()
                    v.tensor_mul(h2s[:, c, :], sg[:, 0, :], rstd_b[:])
                    v.drain()
                v.drain().then_inc(SM["v_h2"], 1)
                LG_CNT = [3, 4, 6, 8]
                CWT_CNT = [5, 7, 9, 10]
                for t1 in range(4):
                    bank = t1 % 2
                    v.wait_ge(SM["p_ms"], base + LG_CNT[t1])
                    v.tensor_copy(lg_sb[:], ps_ms[bank][:, 0:8])
                    v.drain()
                    v.reduce_max(gcl[:, 0:1], lg_sb[:], AX.X, negate=True)
                    v.drain().then_inc(SM["v_gt1"], 1)
                    v.wait_ge(SM["a_g"], 4 * g5 + t1 + 1)
                    v.reduce_sum(gcl[:, 1:2], e_g[:], AX.X)
                    v.drain()
                    v.reciprocal(gcl[:, 2:3], gcl[:, 1:2])
                    v.drain()
                    v.tensor_scalar_mul(lg_sb[:], e_g[:], gcl[:, 2:3])
                    v.drain()
                    v.reduce_max(gcl[:, 3:4], lg_sb[:], AX.X)
                    v.drain()
                    v.tensor_scalar(k_g[:], lg_sb[:], gcl[:, 3:4], None, ALU.is_ge)
                    v.drain()
                    v.tensor_sub(e_g[:], lg_sb[:], k_g[:])
                    v.drain()
                    v.reduce_max(gcl[:, 4:5], e_g[:], AX.X)
                    v.drain()
                    v.tensor_tensor(gcl[:, 5:6], gcl[:, 3:4], gcl[:, 4:5], ALU.add)
                    v.drain()
                    v.reciprocal(gcl[:, 6:7], gcl[:, 5:6])
                    v.drain()
                    v.tensor_scalar(k_g[:], lg_sb[:], gcl[:, 4:5], None, ALU.is_ge)
                    v.drain()
                    if t1 >= 1:
                        v.wait_ge(SM["p_ms"], base + CWT_CNT[t1 - 1])
                    v.scalar_tensor_tensor(cw_g[:], lg_sb[:], gcl[:, 6:7], k_g[:],
                                           ALU.mult, ALU.mult)
                    v.drain().then_inc(SM["v_gt2"], 1)
                    v.wait_ge(SM["p_ms"], base + CWT_CNT[t1])
                    v.tensor_copy(cwt[0:8, t1 * 128:(t1 + 1) * 128],
                                  ps_ms[bank][0:8, 0:128])
                    v.drain().then_inc(SM["v_cw"], 1)
                for m in range(NM):
                    GM = g5 * NM + m
                    if m >= 1:
                        v.wait_ge(SM["p_ms"], base + 10 + m)
                        v.tensor_copy(cbc[:], ps_ms[(m - 1) % 2][:])
                        v.drain().then_inc(SM["v_cb"], 1)
                    for i in range(NI):
                        GP = GM * NI + i
                        v.wait_ge(SM["a_sil"], GP + 1)
                        if i == 0 and GM >= 1:
                            v.wait_ge(SM["p_y"], GM * 8)
                        v.tensor_mul(a_t[:, i, :], sg[:, GP % 2, :],
                                     ps_gu[(GP % 2) * 2 + 1][:]).then_inc(SM["v_a"], 1)
                    for hc in range(8):
                        GY = GM * 8 + hc
                        v.wait_ge(SM["p_y"], GY + 1)
                        if m == 0:
                            v.tensor_add(x_sb[:, hc, GSL], x_sb[:, hc, GSL],
                                         ps_y[GY % 2][:]).then_inc(SM["v_y"], 1)
                        else:
                            v.tensor_mul(mean_b[:], ps_y[GY % 2][:], cbc[:])
                            v.drain()
                            v.tensor_add(x_sb[:, hc, GSL], x_sb[:, hc, GSL],
                                         mean_b[:]).then_inc(SM["v_y"], 1)

        @block.gpsimd
        def _(g):
            g.dma_start(out=x_sb[:], in_=x_in).then_inc(SM["s_x"], 16)
            g.dma_start(out=idb[:], in_=const_in[:, C_ID:C_ID + 128]).then_inc(SM["s_kb"], 16)
            g.dma_start(out=maskb[:], in_=const_in[:, C_MASK:C_MASK + 512]).then_inc(SM["s_kb"], 16)
            g.dma_start(out=onescol_b[:], in_=const_in[:, C_ONESCOL:C_ONESCOL + 1],
                        allow_slow_non_contiguous=True).then_inc(SM["s_kb"], 16)
            g.dma_start(out=selb[:], in_=const_in[0:8, C_SEL:C_SEL + 1024]).then_inc(SM["s_kb"], 16)
            g.dma_start(out=gimgb[:], in_=gate_in).then_inc(SM["s_kb"], 16)
            if PHASE_LIMIT < 3:
                return
            emit_precast_loads(g)

        @block.sync
        def _(sy):
            sy.dma_start(out=kf[:, 0:128], in_=const_in[:, 0:128]).then_inc(SM["s_k"], 16)
            sy.dma_start(out=kf[:, 128:260], in_=const_in[:, 256:388]).then_inc(SM["s_k"], 16)
            if PHASE_LIMIT >= 3:
                emit_precast_stores(sy)
            if PHASE_LIMIT < 3:
                if PHASE_LIMIT == 0:
                    sy.wait_ge(SM["s_x"], 16)
                    sy.wait_ge(SM["s_kb"], 80)
                elif PHASE_LIMIT == 1:
                    if P1_LEVEL == 1:
                        sy.wait_ge(SM["v1"], 64)
                    elif P1_LEVEL == 2:
                        sy.wait_ge(SM["pxs"], 128)
                    elif P1_LEVEL == 3:
                        sy.wait_ge(SM["v2"], 16)
                    else:
                        sy.wait_ge(SM["v3"], 8)
                else:
                    sy.wait_ge(SM["v_xup"], 128)
                nso = 0
                for g5 in range(NG5):
                    for c in range(8):
                        sy.dma_start(out=y_out[:, c, G5SL[g5]],
                                     in_=x_sb[:, c, G5SL[g5]]).then_inc(SM["s_out"], 16)
                        nso += 1
                sy.wait_ge(SM["s_out"], 16 * nso)
                return
            def pc_wait(chunk_idx):
                sy.wait_ge(SM["s_st0"], 16 * PC_CUM[chunk_idx])

            for g5 in range(NG5):
                def issue_wgu(m_, i):
                    GP = (g5 * NM + m_) * NI + i
                    if g5 == 0:
                        pc_wait(m_ * 38 + i)
                    if GP == 0:
                        sy.wait_ge(SM["v_xup"], 128)
                    if GP >= 3:
                        sy.wait_ge(SM["p_gu"], GP - 2)
                    sy.dma_start(out=wgu_buf[:, GP % 3, :],
                                 in_=wgu_bf[m_, :, i, :]).then_inc(
                                     SM[f"s_wg{GP % 3}"], 16)

                for m in range(NM):
                    GM = g5 * NM + m
                    if m == 0:
                        issue_wgu(m, 0)
                    for i in range(1, NI):
                        issue_wgu(m, i)
                    if m < NM - 1:
                        issue_wgu(m + 1, 0)
                    for hc in range(8):
                        for half in range(2):
                            WH = 2 * (GM * 8 + hc) + half
                            if g5 == 0:
                                pc_wait(m * 38 + 22 + 2 * hc + half)
                            if WH >= 4:
                                sy.wait_ge(SM["p_y"], (WH - 4) // 2 + 1)
                            sy.dma_start(out=wd_buf[:, WH % 4, :],
                                         in_=wd_bf[m, :, hc, half * 1408:(half + 1) * 1408]
                                         ).then_inc(SM[f"s_wd{WH % 4}"], 16)
                    if m == 0 and g5 >= 1:
                        pg = g5 - 1
                        sy.wait_ge(SM["v_y"], (pg * NM + NM) * 8)
                        for c in range(8):
                            sy.dma_start(out=y_out[:, c, G5SL[pg]],
                                         in_=x_sb[:, c, G5SL[pg]]).then_inc(SM["s_out"], 16)
            sy.wait_ge(SM["v_y"], (7 * NM + NM) * 8)
            for c in range(8):
                sy.dma_start(out=y_out[:, c, G5SL[7]],
                             in_=x_sb[:, c, G5SL[7]]).then_inc(SM["s_out"], 16)
            sy.wait_ge(SM["s_out"], 16 * 64)

    es.close()
    return nc


_BUILT = {}


def _get_nc():
    if "nc" not in _BUILT:
        _BUILT["nc"] = build_nc()
    return _BUILT["nc"]


def _host_images(L0_attn_x, L0_ffn_gate_w, L0_ffn_base_wgu, L0_ffn_base_wd,
                 L0_ffn_exp_wgu, L0_ffn_exp_wd):
    x = np.asarray(L0_attn_x, dtype=np.float32)
    gate_w = np.asarray(L0_ffn_gate_w, dtype=np.float32)
    base_wgu = np.asarray(L0_ffn_base_wgu, dtype=np.float32)
    base_wd = np.asarray(L0_ffn_base_wd, dtype=np.float32)
    exp_wgu = np.asarray(L0_ffn_exp_wgu, dtype=np.float32)
    exp_wd = np.asarray(L0_ffn_exp_wd, dtype=np.float32)

    x_imgs = []
    for b in range(NCORE):
        xi = np.ascontiguousarray(
            x[b].T.reshape(NCH, 128, S).transpose(1, 0, 2))
        x_imgs.append(xi)

    wgu_img = np.empty((NM, 128, NI, 2048), np.float32)
    wd_img = np.empty((NM, 128, 8, 2816), np.float32)
    for m in range(NM):
        wgu = base_wgu if m == 0 else exp_wgu[m - 1]
        wd = base_wd if m == 0 else exp_wd[m - 1]
        w4 = wgu.reshape(8, 128, 5632)
        gpart = w4[:, :, :2816].reshape(8, 128, NI, 128)
        upart = w4[:, :, 2816:].reshape(8, 128, NI, 128)
        cat = np.concatenate([gpart, upart], axis=3)
        wgu_img[m] = cat.transpose(1, 2, 0, 3).reshape(128, NI, 2048)
        wd4 = wd.reshape(NI, 128, 8, 128)
        wd_img[m] = wd4.transpose(1, 2, 0, 3).reshape(128, 8, 2816)

    gate_img = np.ascontiguousarray(
        gate_w.reshape(8, 128, 8).transpose(1, 0, 2).reshape(128, 64))

    cimg = np.zeros((128, C_W), np.float32)
    cimg[:, C_ID:C_ID + 128] = np.eye(128, dtype=np.float32)
    kk, jj = np.meshgrid(np.arange(128), np.arange(128), indexing="ij")
    cimg[:, C_MASK:C_MASK + 128] = np.where(kk > jj, -1000.0, 0.0)
    cimg[:, C_ONESCOL] = 1.0
    cimg[0, C_ONESROW:C_ONESROW + 128] = 1.0
    cimg[0, C_EPS] = 1e-5
    cimg[0, C_NEGHALF] = -0.5
    cimg[0, C_HALF] = 0.5
    sel = np.zeros((8, 1024), np.float32)
    for e in range(8):
        sel[e, e * 128:(e + 1) * 128] = 1.0
    cimg[0:8, C_SEL:C_SEL + 1024] = sel

    return x_imgs, wgu_img, wd_img, gate_img, cimg


def kernel(L0_attn_x, L0_ffn_gate_w, L0_ffn_base_wgu, L0_ffn_base_wd,
           L0_ffn_exp_wgu, L0_ffn_exp_wd):
    x_imgs, wgu_img, wd_img, gate_img, cimg = _host_images(
        L0_attn_x, L0_ffn_gate_w, L0_ffn_base_wgu, L0_ffn_base_wd,
        L0_ffn_exp_wgu, L0_ffn_exp_wd)
    nc = _get_nc()
    in_maps = [{
        "x_img": x_imgs[b],
        "wgu_img": wgu_img,
        "wd_img": wd_img,
        "gate_img": gate_img,
        "const_img": cimg,
    } for b in range(NCORE)]
    res = run_bass_kernel_spmd(nc, in_maps, core_ids=list(range(NCORE)))
    out = np.empty((NCORE, S, D), np.float32)
    for b in range(NCORE):
        yi = np.asarray(res.results[b]["y_img"], np.float32)
        out[b] = yi.transpose(1, 0, 2).reshape(D, S).T
    return out


# version 23
# speedup vs baseline: 1.1627x; 1.0335x over previous
import numpy as np
import ml_dtypes
from contextlib import ExitStack

import concourse.bass as bass
import concourse.mybir as mybir
from concourse.bass_utils import run_bass_kernel_spmd

F32 = mybir.dt.float32
BF16 = mybir.dt.bfloat16
AF = mybir.ActivationFunctionType
ALU = mybir.AluOpType
AX = mybir.AxisListType

NCORE = 8
S = 4096
D = 1024
NCH = 8
NH = 16
NQ = 8
NI = 22
NM = 9
NG5 = 8
PHASE_LIMIT = 3
P1_LEVEL = 4

def kbs_of(Q):
    return 4 * (Q + 1)

TPH = sum(kbs_of(Q) for Q in range(NQ))
QB = [0]
for _q in range(NQ):
    QB.append(QB[-1] + kbs_of(_q))

def gti(h, Q, kb):
    return h * TPH + QB[Q] + kb

NTILE = NH * TPH

C_ID = 0
C_TRI = 128
C_ONESCOL = 256
C_ONESROW = 257
C_EPS = 385
C_NEGHALF = 386
C_HALF = 387
C_SEL = 392
C_MASK = C_SEL + 1024
C_W = C_MASK + 512


def build_nc():
    nc = bass.Bass("TRN2", target_bir_lowering=False)

    x_in = nc.dram_tensor("x_img", [128, NCH, S], F32, kind="ExternalInput").ap()
    y_out = nc.dram_tensor("y_img", [128, NCH, S], F32, kind="ExternalOutput").ap()
    wgu_in = nc.dram_tensor("wgu_img", [NM, 128, NI, 2048], F32, kind="ExternalInput").ap()
    wd_in = nc.dram_tensor("wd_img", [NM, 128, 8, 2816], F32, kind="ExternalInput").ap()
    gate_in = nc.dram_tensor("gate_img", [128, 64], F32, kind="ExternalInput").ap()
    const_in = nc.dram_tensor("const_img", [128, C_W], F32, kind="ExternalInput").ap()
    wgu_bf = nc.dram_tensor("wgu_bf", [NM, 128, NI, 2048], BF16, kind="Internal").ap()
    wd_bf = nc.dram_tensor("wd_bf", [NM, 128, 8, 2816], BF16, kind="Internal").ap()

    es = ExitStack()
    sb = es.enter_context

    kf = sb(nc.sbuf_tensor("kf", [128, 260], F32))
    idb = sb(nc.sbuf_tensor("idb", [128, 128], BF16))
    maskb = sb(nc.sbuf_tensor("maskb", [128, 512], BF16))
    onescol_b = sb(nc.sbuf_tensor("onescol_b", [128, 1], BF16))
    selb = sb(nc.sbuf_tensor("selb", [8, 1024], BF16))
    gimgb = sb(nc.sbuf_tensor("gimgb", [128, 64], BF16))
    x_sb = sb(nc.sbuf_tensor("x_sb", [128, NCH, S], F32))
    bounce = sb(nc.sbuf_tensor("bounce", [128, 3, 512], BF16))
    xb = sb(nc.sbuf_tensor("xb", [128, 2, 512], BF16))
    xsq = sb(nc.sbuf_tensor("xsq", [128, 2, 512], BF16))
    RA = sb(nc.sbuf_tensor("RA", [33, 512], F32))
    RB = sb(nc.sbuf_tensor("RB", [33, 512], F32))
    r0 = RA[0:1, :]
    r1 = RA[32:33, :]
    r3v = RB[0:1, :]
    r2 = RB[32:33, :]
    mr_col = sb(nc.sbuf_tensor("mr_col", [128, 32], F32))
    r_col = sb(nc.sbuf_tensor("r_col", [128, 32], F32))

    id_f = kf[:, 0:128]
    id1 = kf[0:1, 0:1]
    ones1x128f = kf[0:1, 129:257]
    ones1x64f = kf[0:1, 129:193]
    eps_cell = kf[0:1, 257:258]
    neghalf_cell = kf[0:1, 258:259]
    half_cell = kf[0:1, 259:260]

    esB = ExitStack()
    r4 = esB.enter_context(nc.sbuf_tensor("r4", [1, 512], F32))
    h_tm = [esB.enter_context(nc.sbuf_tensor(f"h_tm{i}", [128, 32, 130], BF16)) for i in range(2)]
    h65 = [esB.enter_context(nc.sbuf_tensor(f"h65_{i}", [65, S], BF16)) for i in range(2)]
    aug = [esB.enter_context(nc.sbuf_tensor(f"aug{i}", [65, 512], BF16)) for i in range(4)]
    n2row = [esB.enter_context(nc.sbuf_tensor(f"n2row{i}", [1, 512], BF16)) for i in range(2)]
    sqt = [esB.enter_context(nc.sbuf_tensor(f"sqt{i}", [64, 512], BF16)) for i in range(2)]
    pt = [esB.enter_context(nc.sbuf_tensor(f"pt{i}", [128, 512], BF16)) for i in range(5)]
    o_sb = [esB.enter_context(nc.sbuf_tensor(f"o_sb{i}", [65, 512], F32)) for i in range(2)]
    t1t = esB.enter_context(nc.sbuf_tensor("t1t", [128, 512], F32))
    rlr = [esB.enter_context(nc.sbuf_tensor(f"rlr{i}", [1, 512], F32)) for i in range(2)]
    cg = esB.enter_context(nc.sbuf_tensor("cg", [1, 8], F32))
    esB.close()

    esC = ExitStack()
    a_t = esC.enter_context(nc.sbuf_tensor("a_t", [128, NI, 512], BF16))
    h2s = esC.enter_context(nc.sbuf_tensor("h2s", [128, NCH, 512], BF16))
    wgu_buf = esC.enter_context(nc.sbuf_tensor("wgu_buf", [128, 3, 2048], BF16))
    wd_buf = esC.enter_context(nc.sbuf_tensor("wd_buf", [128, 4, 1408], BF16))
    sg = esC.enter_context(nc.sbuf_tensor("sg", [128, 2, 512], F32))
    cwt = esC.enter_context(nc.sbuf_tensor("cwt", [8, 512], BF16))
    cbc = esC.enter_context(nc.sbuf_tensor("cbc", [128, 512], F32))
    mean_b = esC.enter_context(nc.sbuf_tensor("mean_b", [128, 512], F32))
    rstd_b = esC.enter_context(nc.sbuf_tensor("rstd_b", [128, 512], F32))
    lg_sb = esC.enter_context(nc.sbuf_tensor("lg_sb", [128, 8], F32))
    e_g = esC.enter_context(nc.sbuf_tensor("e_g", [128, 8], F32))
    k_g = esC.enter_context(nc.sbuf_tensor("k_g", [128, 8], F32))
    cw_g = esC.enter_context(nc.sbuf_tensor("cw_g", [128, 8], F32))
    gcl = esC.enter_context(nc.sbuf_tensor("gcl", [128, 8], F32))
    esC.close()

    esP1 = ExitStack()
    ps_m1 = esP1.enter_context(nc.psum_tensor("ps_m1", [1, 512], F32))
    ps_m2 = esP1.enter_context(nc.psum_tensor("ps_m2", [1, 512], F32))
    ps_tc = esP1.enter_context(nc.psum_tensor("ps_tc", [128, 8], F32))
    esP1.close()
    esP2 = ExitStack()
    ps_s = [esP2.enter_context(nc.psum_tensor(f"ps_s{i}", [128, 512], F32)) for i in range(3)]
    ps_o = [esP2.enter_context(nc.psum_tensor(f"ps_o{i}", [65, 512], F32)) for i in range(2)]
    ps_bx = esP2.enter_context(nc.psum_tensor("ps_bx", [128, 512], F32))
    ps_bh = esP2.enter_context(nc.psum_tensor("ps_bh", [64, 512], BF16))
    ps_nrl = esP2.enter_context(nc.psum_tensor("ps_nrl", [65, 512], F32))
    esP2.close()
    esP3 = ExitStack()
    ps_gu = [esP3.enter_context(nc.psum_tensor(f"ps_gu{i}", [128, 512], F32)) for i in range(4)]
    ps_y = [esP3.enter_context(nc.psum_tensor(f"ps_y{i}", [128, 512], F32)) for i in range(2)]
    ps_ms = [esP3.enter_context(nc.psum_tensor(f"ps_ms{i}", [128, 512], F32)) for i in range(2)]
    esP3.close()

    sem_names = [
        "s_x", "s_k", "s_kb", "s_ld0", "s_st0",
        "s_wg0", "s_wg1", "s_wg2",
        "s_wd0", "s_wd1", "s_wd2", "s_wd3", "s_out",
        "a1", "v1", "pxs", "a2", "v2", "p1t", "v3",
        "a_sq", "p_n2", "v_aug", "p_s", "a_exp", "p_pv",
        "v_rl", "p_rlb", "v_xup", "p_bx", "v_bx", "p_bh", "v_bh",
        "a13", "v13", "pxs3", "a23", "v23", "v_st", "p_ms", "v_gt1",
        "a_g", "v_gt2", "v_cw", "p_gu", "a_sil", "v_a", "p_y", "v_y",
        "v_cb", "v_h2",
    ]
    SM = {n: sb(nc.semaphore(n)) for n in sem_names}

    def mean_cnt(ci):
        return 2 * ci + 1

    def sq_cnt(ci):
        return 2 * ci + 2

    def head_builds(h):
        out = []
        if h >= 15:
            return out
        if h % 2 == 1:
            p = (h + 1) // 2
            out += [("bx", p, g) for g in range(8)]
        out += [("bh", h + 1, g) for g in range(8)]
        return out

    G5SL = [slice(g5 * 512, (g5 + 1) * 512) for g5 in range(NG5)]

    PC_PIECES = []
    PC_CUM = []
    _cum = 0
    for _m in range(NM):
        for _i in range(NI):
            for _qq in range(4):
                PC_PIECES.append(("wgu", _m, _i, 0, _qq * 512, 512))
            _cum += 4
            PC_CUM.append(_cum)
        for _hc in range(8):
            for _half in range(2):
                for _off, _w in ((0, 512), (512, 512), (1024, 384)):
                    PC_PIECES.append(("wd", _m, _hc, _half, _off, _w))
                _cum += 3
                PC_CUM.append(_cum)

    def _pc_aps(piece):
        kind, m_, a, b, off, w = piece
        if kind == "wgu":
            return (wgu_in[m_, :, a, off:off + w],
                    wgu_bf[m_, :, a, off:off + w], w)
        cs = b * 1408 + off
        return (wd_in[m_, :, a, cs:cs + w],
                wd_bf[m_, :, a, cs:cs + w], w)

    def emit_precast_loads(eng):
        for p, piece in enumerate(PC_PIECES):
            src_ap, _, w = _pc_aps(piece)
            if p >= 3:
                eng.wait_ge(SM["s_st0"], 16 * (p - 2))
            if PHASE_LIMIT >= 2 and 2 * p - 300 > 0:
                eng.wait_ge(SM["a_exp"], min(2 * p - 300, NTILE))
            eng.dma_start(out=bounce[:, p % 3, 0:w],
                          in_=src_ap).then_inc(SM["s_ld0"], 16)

    def emit_precast_stores(eng):
        for p, piece in enumerate(PC_PIECES):
            _, dst_ap, w = _pc_aps(piece)
            eng.wait_ge(SM["s_ld0"], 16 * (p + 1))
            eng.dma_start(out=dst_ap,
                          in_=bounce[:, p % 3, 0:w]).then_inc(SM["s_st0"], 16)

    with nc.Block() as block:

        @block.tensor
        def _(t):
            if PHASE_LIMIT < 1 or P1_LEVEL < 2:
                if PHASE_LIMIT >= 1 and P1_LEVEL >= 2:
                    pass
                if PHASE_LIMIT < 1 or P1_LEVEL < 2:
                    return
            for tt in range(8):
                for c in range(8):
                    ci = 8 * tt + c
                    t.wait_ge(SM["a1"], ci + 1)
                    if ci == 0:
                        t.wait_ge(SM["s_kb"], 16 * 5)
                        t.wait_ge(SM["s_k"], 32)
                    if tt > 0 and c == 0 and P1_LEVEL >= 3:
                        t.wait_ge(SM["a2"], 2 * (tt - 1) + 1)
                    t.matmul(ps_m1[:], onescol_b[:], xb[:, c % 2, :],
                             start=(c == 0), stop=(c == 7),
                             skip_group_check=True).then_inc(SM["pxs"], 1)
                    t.wait_ge(SM["v1"], ci + 1)
                    t.matmul(ps_m2[:], onescol_b[:], xsq[:, c % 2, :],
                             start=(c == 0), stop=(c == 7),
                             skip_group_check=True).then_inc(SM["pxs"], 1)
                if P1_LEVEL < 4:
                    continue
                t.wait_ge(SM["v2"], 2 * tt + 2)
                if tt > 0:
                    t.wait_ge(SM["v3"], tt)
                for j in range(4):
                    t.transpose(ps_tc[:, j:j + 1], r4[0:1, j * 128:(j + 1) * 128], id1)
                for j in range(4):
                    mmtc = t.transpose(ps_tc[:, 4 + j:5 + j], r3v[0:1, j * 128:(j + 1) * 128], id1)
                mmtc.then_inc(SM["p1t"], 1)

            if PHASE_LIMIT < 2:
                return
            def pe_build(kind, pidx, g):
                if kind == "bx":
                    if 8 * pidx + g >= 1:
                        t.wait_ge(SM["v_bx"], 8 * pidx + g)
                    for j in range(4):
                        tc = g * 4 + j
                        mm = t.transpose(ps_bx[:, j * 128:(j + 1) * 128],
                                         x_sb[:, pidx, tc * 128:(tc + 1) * 128], id_f)
                    mm.then_inc(SM["p_bx"], 1)
                else:
                    hb = pidx
                    t.wait_ge(SM["v_bh"], 8 * hb + g)
                    t.wait_ge(SM["v_bx"], 8 * (hb // 2) + g + 1)
                    for j in range(4):
                        tc = g * 4 + j
                        mm = t.transpose(ps_bh[0:64, j * 128:(j + 1) * 128],
                                         h_tm[(hb // 2) % 2][:, tc, 65 * (hb % 2):65 * (hb % 2) + 64],
                                         idb[:])
                    mm.then_inc(SM["p_bh"], 1)

            for g in range(8):
                pe_build("bx", 0, g)
            for g in range(8):
                pe_build("bh", 0, g)

            for h in range(NH):
                pair = h // 2
                builds = head_builds(h)
                bpos = 0
                def pe_n2(Qn):
                    AQn = 8 * h + Qn
                    t.wait_ge(SM["a_sq"], AQn + 1)
                    if AQn >= 1:
                        t.wait_ge(SM["v_aug"], AQn)
                    t.matmul(ps_nrl[64:65, :], onescol_b[0:64, :], sqt[Qn % 2][:],
                             start=True, stop=True,
                             skip_group_check=True).then_inc(SM["p_n2"], 1)

                def emit_pv(Q, kb):
                    T = gti(h, Q, kb)
                    AQ = 8 * h + Q
                    nkb = kbs_of(Q)
                    sk = max(128 * kb - 512 * Q, 0)
                    t.wait_ge(SM["a_exp"], T + 1)
                    if kb == 0:
                        if AQ >= 2:
                            t.wait_ge(SM["v_rl"], AQ - 1)
                        if Q == 0:
                            t.wait_ge(SM["v_bx"], 8 * (pair + 1))
                    t.matmul(ps_o[AQ % 2][0:65, sk:512],
                             h_tm[pair % 2][:, kb, 65 * (h % 2):65 * (h % 2) + 65],
                             pt[T % 5][:, sk:512],
                             start=(kb == 0), stop=(kb == nkb - 1),
                             skip_group_check=True).then_inc(SM["p_pv"], 1)

                pe_n2(0)
                pe_n2(1)
                pend = []
                defer = []
                for Q in range(NQ):
                    AQ = 8 * h + Q
                    nkb = kbs_of(Q)
                    rlb_slot = 3 if Q == 0 else 5
                    for kb in range(nkb):
                        T = gti(h, Q, kb)
                        delta = 128 * kb - 512 * Q
                        sk = max(delta, 0)

                        if kb == rlb_slot and AQ >= 1:
                            pv = AQ - 1
                            t.wait_ge(SM["v_rl"], pv + 1)
                            if pv >= 1:
                                t.wait_ge(SM["v_xup"], pv)
                            t.matmul(ps_nrl[0:64, :], ones1x64f, rlr[pv % 2][:],
                                     start=True, stop=True,
                                     skip_group_check=True).then_inc(SM["p_rlb"], 1)
                        if kb == 0 and Q == 0:
                            t.wait_ge(SM["v_bh"], 8 * (h + 1))
                        if kb == 0:
                            t.wait_ge(SM["v_aug"], AQ + 1)
                        if T >= 3:
                            t.wait_ge(SM["a_exp"], T - 2)
                        diag = delta >= 0
                        mm = t.matmul(ps_s[T % 3][:, sk:512],
                                      h65[h % 2][0:65, kb * 128:(kb + 1) * 128],
                                      aug[Q % 4][0:65, sk:512],
                                      start=True, stop=(not diag),
                                      skip_group_check=True)
                        if diag:
                            mm = t.matmul(ps_s[T % 3][:, sk:512],
                                          idb[:], maskb[:, 0:512 - sk],
                                          start=False, stop=True,
                                          skip_group_check=True)
                        mm.then_inc(SM["p_s"], 1)
                        pend.append((Q, kb))
                        if len(pend) > 3:
                            emit_pv(*pend.pop(0))
                        if kb == 2:
                            for job in defer:
                                job()
                            defer = []
                    if Q + 2 <= 7:
                        defer.append(lambda Qn=Q + 2: pe_n2(Qn))
                    for _ in range(2 if len(builds) > 8 else 1):
                        if bpos < len(builds):
                            b = builds[bpos]; bpos += 1
                            defer.append(lambda b=b: pe_build(*b))
                while pend:
                    emit_pv(*pend.pop(0))
                for job in defer:
                    job()
                defer = []
                while bpos < len(builds):
                    pe_build(*builds[bpos]); bpos += 1
            t.wait_ge(SM["v_rl"], 128)
            t.wait_ge(SM["v_xup"], 127)
            t.matmul(ps_nrl[0:64, :], ones1x64f, rlr[127 % 2][:],
                     start=True, stop=True,
                     skip_group_check=True).then_inc(SM["p_rlb"], 1)

            if PHASE_LIMIT < 3:
                return
            for g5 in range(NG5):
                base = 18 * g5
                for c in range(8):
                    ci = 8 * g5 + c
                    t.wait_ge(SM["a13"], ci + 1)
                    if c == 0 and g5 >= 1:
                        t.wait_ge(SM["v_cb"], 8 * (g5 - 1) + 7)
                        t.wait_ge(SM["v_cb"], 8 * (g5 - 1) + 8)
                    t.matmul(ps_ms[0][0:1, :], onescol_b[:], xb[:, c % 2, :],
                             start=(c == 0), stop=(c == 7),
                             skip_group_check=True).then_inc(SM["pxs3"], 1)
                    t.wait_ge(SM["v13"], ci + 1)
                    t.matmul(ps_ms[1][0:1, :], onescol_b[:], xsq[:, c % 2, :],
                             start=(c == 0), stop=(c == 7),
                             skip_group_check=True).then_inc(SM["pxs3"], 1)
                t.wait_ge(SM["v23"], 2 * g5 + 2)
                t.wait_ge(SM["a23"], 2 * g5 + 1)
                t.matmul(ps_ms[0][:], ones1x128f, r0,
                         start=True, stop=True).then_inc(SM["p_ms"], 1)
                t.matmul(ps_ms[1][:], ones1x128f, r3v,
                         start=True, stop=True).then_inc(SM["p_ms"], 1)
                def pe_lg(t1):
                    bank = t1 % 2
                    if t1 == 0:
                        t.wait_ge(SM["v_st"], 2 * g5 + 1)
                        t.wait_ge(SM["v_h2"], g5 + 1)
                    elif t1 == 1:
                        t.wait_ge(SM["v_st"], 2 * g5 + 2)
                    else:
                        t.wait_ge(SM["v_cw"], 4 * g5 + t1 - 1)
                    for kc in range(8):
                        mm = t.matmul(ps_ms[bank][:, 0:8],
                                      h2s[:, kc, t1 * 128:(t1 + 1) * 128],
                                      gimgb[:, kc * 8:(kc + 1) * 8],
                                      start=(kc == 0), stop=(kc == 7))
                    mm.then_inc(SM["p_ms"], 1)

                def pe_cwt(t1):
                    bank = t1 % 2
                    t.wait_ge(SM["v_gt1"], 4 * g5 + t1 + 1)
                    t.wait_ge(SM["v_gt2"], 4 * g5 + t1 + 1)
                    t.transpose(ps_ms[bank][0:8, 0:128], cw_g[:, 0:8],
                                id_f).then_inc(SM["p_ms"], 1)

                pe_lg(0)
                pe_lg(1)
                pe_cwt(0)
                pe_lg(2)
                pe_cwt(1)
                pe_lg(3)
                pe_cwt(2)
                pe_cwt(3)
                def emit_gu(m_, i):
                    GP = (g5 * NM + m_) * NI + i
                    t.wait_ge(SM[f"s_wg{GP % 3}"], 16 * (GP // 3 + 1))
                    if GP >= 2:
                        t.wait_ge(SM["v_a"], GP - 1)
                    if i == 0:
                        t.wait_ge(SM["v_h2"], g5 + 1)
                    bg = (GP % 2) * 2
                    bu = bg + 1
                    for k in range(8):
                        t.matmul(ps_gu[bg][:], wgu_buf[:, GP % 3, k * 256:k * 256 + 128],
                                 h2s[:, k, :], start=(k == 0), stop=(k == 7))
                    for k in range(8):
                        mm = t.matmul(ps_gu[bu][:], wgu_buf[:, GP % 3, k * 256 + 128:k * 256 + 256],
                                      h2s[:, k, :], start=(k == 0), stop=(k == 7))
                    mm.then_inc(SM["p_gu"], 1)

                for m in range(NM):
                    GM = g5 * NM + m
                    if m >= 1:
                        bank = (m - 1) % 2
                        if m <= 2:
                            t.wait_ge(SM["v_cw"], 4 * g5 + 2 + m)
                        else:
                            t.wait_ge(SM["v_cb"], 8 * g5 + m - 2)
                        t.matmul(ps_ms[bank][:], selb[:, (m - 1) * 128:m * 128],
                                 cwt[0:8, :], start=True, stop=True).then_inc(SM["p_ms"], 1)
                    for i in (range(NI) if m == 0 else range(1, NI)):
                        emit_gu(m, i)
                    if m < NM - 1:
                        emit_gu(m + 1, 0)
                    for hc in range(8):
                        GY = GM * 8 + hc
                        if hc == 0:
                            t.wait_ge(SM["v_a"], GM * NI + NI)
                        if GY >= 2:
                            t.wait_ge(SM["v_y"], GY - 1)
                        for half in range(2):
                            WH = 2 * GY + half
                            t.wait_ge(SM[f"s_wd{WH % 4}"], 16 * (WH // 4 + 1))
                            for il in range(11):
                                ic = half * 11 + il
                                mm = t.matmul(ps_y[GY % 2][:],
                                              wd_buf[:, WH % 4, il * 128:(il + 1) * 128],
                                              a_t[:, ic, :],
                                              start=(ic == 0), stop=(ic == NI - 1))
                        mm.then_inc(SM["p_y"], 1)

        @block.scalar
        def _(s):
            if PHASE_LIMIT < 1:
                return
            for tt in range(8):
                for c in range(8):
                    ci = 8 * tt + c
                    if ci == 0:
                        s.wait_ge(SM["s_x"], 16)
                    if ci >= 2:
                        s.wait_ge(SM["v1"], ci - 1)
                        if P1_LEVEL >= 2:
                            s.wait_ge(SM["pxs"], mean_cnt(ci - 2))
                    s.copy(xb[:, c % 2, :],
                           x_sb[:, c, tt * 512:(tt + 1) * 512]).then_inc(SM["a1"], 1)
                if P1_LEVEL < 3:
                    continue
                s.wait_ge(SM["pxs"], 16 * tt + 16)
                if tt >= 1:
                    s.wait_ge(SM["v2"], 2 * (tt - 1) + 2)
                s.mul(r0, ps_m1[0:1, :], 1.0 / 1024.0)
                s.mul(r1, ps_m2[0:1, :], 1.0 / 1024.0)
                s.drain().then_inc(SM["a2"], 1)
                if ci == 7 and tt == 0:
                    s.wait_ge(SM["s_k"], 32)
                s.wait_ge(SM["v2"], 2 * tt + 1)
                if P1_LEVEL >= 4:
                    s.wait_ge(SM["p1t"], tt)
                s.activation(r2, r1, AF.Sqrt, bias=eps_cell)
                s.drain().then_inc(SM["a2"], 1)

            if PHASE_LIMIT < 2:
                return
            for h in range(NH):
                for Q in range(NQ):
                    nkb = kbs_of(Q)
                    for kb in range(nkb):
                        g = gti(h, Q, kb)
                        sk = max(128 * kb - 512 * Q, 0)
                        s.wait_ge(SM["p_s"], g + 1)
                        if g >= 5:
                            s.wait_ge(SM["p_pv"], g - 4)
                        s.activation(pt[g % 5][:, sk:512], ps_s[g % 3][:, sk:512],
                                     AF.Exp, scale=0.125).then_inc(SM["a_exp"], 1)

            if PHASE_LIMIT < 3:
                return
            for g5 in range(NG5):
                base = 18 * g5
                GSL = G5SL[g5]
                for c in range(8):
                    ci = 8 * g5 + c
                    if ci == 0:
                        s.wait_ge(SM["v_xup"], 128)
                    if ci >= 2:
                        s.wait_ge(SM["v13"], ci - 1)
                        s.wait_ge(SM["pxs3"], mean_cnt(ci - 2))
                    s.copy(xb[:, c % 2, :], x_sb[:, c, GSL]).then_inc(SM["a13"], 1)
                s.wait_ge(SM["pxs3"], 16 * g5 + 16)
                if g5 >= 1:
                    s.wait_ge(SM["v23"], 2 * (g5 - 1) + 2)
                    s.wait_ge(SM["p_ms"], 18 * (g5 - 1) + 2)
                s.mul(r0, ps_ms[0][0:1, :], 1.0 / 1024.0)
                s.mul(r1, ps_ms[1][0:1, :], 1.0 / 1024.0)
                s.drain().then_inc(SM["a23"], 1)
                s.wait_ge(SM["v23"], 2 * g5 + 1)
                s.activation(r2, r1, AF.Sqrt, bias=eps_cell)
                s.drain().then_inc(SM["a23"], 1)
                for t1 in range(4):
                    s.wait_ge(SM["v_gt1"], 4 * g5 + t1 + 1)
                    s.activation(e_g[:], lg_sb[:], AF.Exp, bias=gcl[:, 0:1])
                    s.drain().then_inc(SM["a_g"], 1)
                for m in range(NM):
                    GM = g5 * NM + m
                    for i in range(NI):
                        GP = GM * NI + i
                        s.wait_ge(SM["p_gu"], GP + 1)
                        if GP >= 2:
                            s.wait_ge(SM["v_a"], GP - 1)
                        s.activation(sg[:, GP % 2, :], ps_gu[(GP % 2) * 2][:],
                                     AF.Silu).then_inc(SM["a_sil"], 1)

        @block.vector
        def _(v):
            if PHASE_LIMIT < 1:
                return
            for tt in range(8):
                for c in range(8):
                    ci = 8 * tt + c
                    v.wait_ge(SM["a1"], ci + 1)
                    if ci >= 2 and P1_LEVEL >= 2:
                        v.wait_ge(SM["pxs"], sq_cnt(ci - 2))
                    v.tensor_mul(xsq[:, c % 2, :], xb[:, c % 2, :],
                                 xb[:, c % 2, :]).then_inc(SM["v1"], 1)
                if P1_LEVEL < 3:
                    continue
                v.wait_ge(SM["a2"], 2 * tt + 1)
                v.tensor_mul(r2, r0, r0)
                v.drain()
                v.tensor_sub(r1, r1, r2)
                v.drain().then_inc(SM["v2"], 1)
                v.wait_ge(SM["a2"], 2 * tt + 2)
                if tt >= 1 and P1_LEVEL >= 4:
                    v.wait_ge(SM["p1t"], tt)
                v.reciprocal(r3v, r2)
                v.drain()
                v.tensor_mul(r4[0:1, :], r0, r3v)
                v.drain().then_inc(SM["v2"], 1)
                if P1_LEVEL < 4:
                    continue
                v.wait_ge(SM["p1t"], tt + 1)
                v.tensor_copy(mr_col[:, 4 * tt:4 * tt + 4], ps_tc[:, 0:4])
                v.tensor_copy(r_col[:, 4 * tt:4 * tt + 4], ps_tc[:, 4:8])
                v.drain().then_inc(SM["v3"], 1)

            if PHASE_LIMIT < 2:
                return
            v.wait_ge(SM["s_k"], 32)
            v.memset(h65[0][64:65, :], 1.0)
            v.memset(h65[1][64:65, :], 1.0)
            v.drain()

            def dve_build(kind, pidx, g):
                if kind == "bx":
                    v.wait_ge(SM["p_bx"], 8 * pidx + g + 1)
                    v.wait_ge(SM["v3"], g + 1)
                    if g == 0:
                        if pidx >= 2:
                            v.wait_ge(SM["p_pv"], TPH * (2 * pidx - 2))
                        v.memset(h_tm[pidx % 2][:, :, 64:65], 1.0)
                        v.memset(h_tm[pidx % 2][:, :, 129:130], 1.0)
                    for j in range(4):
                        tc = g * 4 + j
                        v.tensor_scalar(h_tm[pidx % 2][:, tc, 0:64],
                                        ps_bx[:, j * 128:j * 128 + 64],
                                        r_col[:, tc:tc + 1], mr_col[:, tc:tc + 1],
                                        ALU.mult, ALU.subtract)
                        v.tensor_scalar(h_tm[pidx % 2][:, tc, 65:129],
                                        ps_bx[:, j * 128 + 64:(j + 1) * 128],
                                        r_col[:, tc:tc + 1], mr_col[:, tc:tc + 1],
                                        ALU.mult, ALU.subtract)
                    v.drain().then_inc(SM["v_bx"], 1)
                else:
                    hb = pidx
                    v.wait_ge(SM["p_bh"], 8 * hb + g + 1)
                    if g == 0 and hb >= 2:
                        v.wait_ge(SM["p_s"], TPH * (hb - 1))
                        v.wait_ge(SM["a_sq"], 8 * (hb - 1))
                    for j in range(4):
                        tc = g * 4 + j
                        v.tensor_copy(h65[hb % 2][0:64, tc * 128:(tc + 1) * 128],
                                      ps_bh[0:64, j * 128:(j + 1) * 128])
                    v.drain().then_inc(SM["v_bh"], 1)

            def dve_sq(h, Qn):
                AQn = 8 * h + Qn
                if Qn == 0:
                    v.wait_ge(SM["v_bh"], 8 * (h + 1))
                if AQn >= 2:
                    v.wait_ge(SM["p_n2"], AQn - 1)
                v.tensor_mul(sqt[Qn % 2][:],
                             h65[h % 2][0:64, Qn * 512:(Qn + 1) * 512],
                             h65[h % 2][0:64, Qn * 512:(Qn + 1) * 512]
                             ).then_inc(SM["a_sq"], 1)

            def dve_pre(h, Q):
                AQ = 8 * h + Q
                v.wait_ge(SM["p_n2"], AQ + 1)
                if Q == 0:
                    v.wait_ge(SM["p_s"], TPH * h)
                v.tensor_copy(n2row[Q % 2][0:1, :], ps_nrl[64:65, :])
                v.drain()
                v.reduce_max(cg[:, 0:1], n2row[Q % 2][0:1, :], AX.X, negate=True)
                v.drain()
                if Q == 0:
                    v.tensor_copy(cg[:, 1 + (Q % 2):2 + (Q % 2)], cg[:, 0:1])
                else:
                    v.tensor_tensor(cg[:, 1 + (Q % 2):2 + (Q % 2)],
                                    cg[:, 1 + ((Q - 1) % 2):2 + ((Q - 1) % 2)],
                                    cg[:, 0:1], ALU.min)
                v.drain()
                v.tensor_scalar_mul(cg[:, 3:4], cg[:, 1 + (Q % 2):2 + (Q % 2)], half_cell)
                v.drain()
                v.tensor_scalar(aug[Q % 4][64:65, :], n2row[Q % 2][0:1, :],
                                neghalf_cell, cg[:, 3:4], ALU.mult, ALU.add)
                v.tensor_copy(aug[Q % 4][0:64, :], h65[h % 2][0:64, Q * 512:(Q + 1) * 512])
                v.drain().then_inc(SM["v_aug"], 1)

            def dve_finB(aq):
                hh, QQ = aq // 8, aq % 8
                rowbase = 64 * (hh % 2)
                c = hh // 2
                v.wait_ge(SM["p_rlb"], aq + 1)
                v.tensor_mul(t1t[rowbase:rowbase + 64, :], o_sb[aq % 2][0:64, :],
                             ps_nrl[0:64, :])
                v.drain()
                v.tensor_add(x_sb[rowbase:rowbase + 64, c, QQ * 512:(QQ + 1) * 512],
                             x_sb[rowbase:rowbase + 64, c, QQ * 512:(QQ + 1) * 512],
                             t1t[rowbase:rowbase + 64, :])
                v.drain().then_inc(SM["v_xup"], 1)

            for g in range(8):
                dve_build("bx", 0, g)
            for g in range(8):
                dve_build("bh", 0, g)

            for h in range(NH):
                builds = head_builds(h)
                bpos = 0
                dve_sq(h, 0)
                dve_pre(h, 0)
                dve_sq(h, 1)
                dve_pre(h, 1)
                for Q in range(NQ):
                    AQ = 8 * h + Q
                    if Q + 2 <= 7:
                        dve_sq(h, Q + 2)
                    v.wait_ge(SM["p_pv"], h * TPH + QB[Q + 1])
                    v.tensor_copy(o_sb[AQ % 2][:], ps_o[AQ % 2][0:65, :])
                    v.drain()
                    v.reciprocal(rlr[AQ % 2][0:1, :], o_sb[AQ % 2][64:65, :])
                    v.drain().then_inc(SM["v_rl"], 1)
                    if AQ >= 1:
                        dve_finB(AQ - 1)
                    if Q + 2 <= 7:
                        dve_pre(h, Q + 2)
                    if bpos < len(builds):
                        dve_build(*builds[bpos]); bpos += 1
                    if bpos < len(builds) and len(builds) > 8:
                        dve_build(*builds[bpos]); bpos += 1
                while bpos < len(builds):
                    dve_build(*builds[bpos]); bpos += 1
            dve_finB(127)

            if PHASE_LIMIT < 3:
                return
            for g5 in range(NG5):
                base = 18 * g5
                GSL = G5SL[g5]
                for c in range(8):
                    ci = 8 * g5 + c
                    v.wait_ge(SM["a13"], ci + 1)
                    if ci >= 2:
                        v.wait_ge(SM["pxs3"], sq_cnt(ci - 2))
                    v.tensor_mul(xsq[:, c % 2, :], xb[:, c % 2, :],
                                 xb[:, c % 2, :]).then_inc(SM["v13"], 1)
                v.wait_ge(SM["a23"], 2 * g5 + 1)
                v.tensor_mul(r2, r0, r0)
                v.drain()
                v.tensor_sub(r1, r1, r2)
                v.drain().then_inc(SM["v23"], 1)
                v.wait_ge(SM["a23"], 2 * g5 + 2)
                if g5 >= 1:
                    v.wait_ge(SM["p_ms"], 18 * (g5 - 1) + 2)
                v.reciprocal(r3v, r2)
                v.drain().then_inc(SM["v23"], 1)
                v.wait_ge(SM["p_ms"], base + 1)
                v.tensor_copy(mean_b[:], ps_ms[0][:])
                v.drain().then_inc(SM["v_st"], 1)
                v.wait_ge(SM["p_ms"], base + 2)
                v.tensor_copy(rstd_b[:], ps_ms[1][:])
                v.drain().then_inc(SM["v_st"], 1)
                if g5 >= 1:
                    v.wait_ge(SM["p_gu"], (g5 * NM) * NI)
                    v.wait_ge(SM["p_ms"], 18 * (g5 - 1) + 8)
                for c in range(8):
                    v.tensor_sub(sg[:, 0, :], x_sb[:, c, GSL], mean_b[:])
                    v.drain()
                    v.tensor_mul(h2s[:, c, :], sg[:, 0, :], rstd_b[:])
                    v.drain()
                v.drain().then_inc(SM["v_h2"], 1)
                LG_CNT = [3, 4, 6, 8]
                CWT_CNT = [5, 7, 9, 10]
                for t1 in range(4):
                    bank = t1 % 2
                    v.wait_ge(SM["p_ms"], base + LG_CNT[t1])
                    v.tensor_copy(lg_sb[:], ps_ms[bank][:, 0:8])
                    v.drain()
                    v.reduce_max(gcl[:, 0:1], lg_sb[:], AX.X, negate=True)
                    v.drain().then_inc(SM["v_gt1"], 1)
                    v.wait_ge(SM["a_g"], 4 * g5 + t1 + 1)
                    v.reduce_sum(gcl[:, 1:2], e_g[:], AX.X)
                    v.drain()
                    v.reciprocal(gcl[:, 2:3], gcl[:, 1:2])
                    v.drain()
                    v.tensor_scalar_mul(lg_sb[:], e_g[:], gcl[:, 2:3])
                    v.drain()
                    v.reduce_max(gcl[:, 3:4], lg_sb[:], AX.X)
                    v.drain()
                    v.tensor_scalar(k_g[:], lg_sb[:], gcl[:, 3:4], None, ALU.is_ge)
                    v.drain()
                    v.tensor_sub(e_g[:], lg_sb[:], k_g[:])
                    v.drain()
                    v.reduce_max(gcl[:, 4:5], e_g[:], AX.X)
                    v.drain()
                    v.tensor_tensor(gcl[:, 5:6], gcl[:, 3:4], gcl[:, 4:5], ALU.add)
                    v.drain()
                    v.reciprocal(gcl[:, 6:7], gcl[:, 5:6])
                    v.drain()
                    v.tensor_scalar(k_g[:], lg_sb[:], gcl[:, 4:5], None, ALU.is_ge)
                    v.drain()
                    if t1 >= 1:
                        v.wait_ge(SM["p_ms"], base + CWT_CNT[t1 - 1])
                    v.scalar_tensor_tensor(cw_g[:], lg_sb[:], gcl[:, 6:7], k_g[:],
                                           ALU.mult, ALU.mult)
                    v.drain().then_inc(SM["v_gt2"], 1)
                    v.wait_ge(SM["p_ms"], base + CWT_CNT[t1])
                    v.tensor_copy(cwt[0:8, t1 * 128:(t1 + 1) * 128],
                                  ps_ms[bank][0:8, 0:128])
                    v.drain().then_inc(SM["v_cw"], 1)
                for m in range(NM):
                    GM = g5 * NM + m
                    if m >= 1:
                        v.wait_ge(SM["p_ms"], base + 10 + m)
                        v.tensor_copy(cbc[:], ps_ms[(m - 1) % 2][:])
                        v.drain().then_inc(SM["v_cb"], 1)
                    for i in range(NI):
                        GP = GM * NI + i
                        v.wait_ge(SM["a_sil"], GP + 1)
                        if i == 0 and GM >= 1:
                            v.wait_ge(SM["p_y"], GM * 8)
                        v.tensor_mul(a_t[:, i, :], sg[:, GP % 2, :],
                                     ps_gu[(GP % 2) * 2 + 1][:]).then_inc(SM["v_a"], 1)
                    for hc in range(8):
                        GY = GM * 8 + hc
                        v.wait_ge(SM["p_y"], GY + 1)
                        if m == 0:
                            v.tensor_add(x_sb[:, hc, GSL], x_sb[:, hc, GSL],
                                         ps_y[GY % 2][:]).then_inc(SM["v_y"], 1)
                        else:
                            v.tensor_mul(mean_b[:], ps_y[GY % 2][:], cbc[:])
                            v.drain()
                            v.tensor_add(x_sb[:, hc, GSL], x_sb[:, hc, GSL],
                                         mean_b[:]).then_inc(SM["v_y"], 1)

        @block.gpsimd
        def _(g):
            g.dma_start(out=x_sb[:], in_=x_in).then_inc(SM["s_x"], 16)
            g.dma_start(out=idb[:], in_=const_in[:, C_ID:C_ID + 128]).then_inc(SM["s_kb"], 16)
            g.dma_start(out=maskb[:], in_=const_in[:, C_MASK:C_MASK + 512]).then_inc(SM["s_kb"], 16)
            g.dma_start(out=onescol_b[:], in_=const_in[:, C_ONESCOL:C_ONESCOL + 1],
                        allow_slow_non_contiguous=True).then_inc(SM["s_kb"], 16)
            g.dma_start(out=selb[:], in_=const_in[0:8, C_SEL:C_SEL + 1024]).then_inc(SM["s_kb"], 16)
            g.dma_start(out=gimgb[:], in_=gate_in).then_inc(SM["s_kb"], 16)
            if PHASE_LIMIT < 3:
                return
            emit_precast_loads(g)

        @block.sync
        def _(sy):
            sy.dma_start(out=kf[:, 0:128], in_=const_in[:, 0:128]).then_inc(SM["s_k"], 16)
            sy.dma_start(out=kf[:, 128:260], in_=const_in[:, 256:388]).then_inc(SM["s_k"], 16)
            if PHASE_LIMIT >= 3:
                emit_precast_stores(sy)
            if PHASE_LIMIT < 3:
                if PHASE_LIMIT == 0:
                    sy.wait_ge(SM["s_x"], 16)
                    sy.wait_ge(SM["s_kb"], 80)
                elif PHASE_LIMIT == 1:
                    if P1_LEVEL == 1:
                        sy.wait_ge(SM["v1"], 64)
                    elif P1_LEVEL == 2:
                        sy.wait_ge(SM["pxs"], 128)
                    elif P1_LEVEL == 3:
                        sy.wait_ge(SM["v2"], 16)
                    else:
                        sy.wait_ge(SM["v3"], 8)
                else:
                    sy.wait_ge(SM["v_xup"], 128)
                nso = 0
                for g5 in range(NG5):
                    for c in range(8):
                        sy.dma_start(out=y_out[:, c, G5SL[g5]],
                                     in_=x_sb[:, c, G5SL[g5]]).then_inc(SM["s_out"], 16)
                        nso += 1
                sy.wait_ge(SM["s_out"], 16 * nso)
                return
            def pc_wait(chunk_idx):
                sy.wait_ge(SM["s_st0"], 16 * PC_CUM[chunk_idx])

            for g5 in range(NG5):
                def issue_wgu(m_, i):
                    GP = (g5 * NM + m_) * NI + i
                    if g5 == 0:
                        pc_wait(m_ * 38 + i)
                    if GP == 0:
                        sy.wait_ge(SM["v_xup"], 128)
                    if GP >= 3:
                        sy.wait_ge(SM["p_gu"], GP - 2)
                    sy.dma_start(out=wgu_buf[:, GP % 3, :],
                                 in_=wgu_bf[m_, :, i, :]).then_inc(
                                     SM[f"s_wg{GP % 3}"], 16)

                for m in range(NM):
                    GM = g5 * NM + m
                    if m == 0:
                        issue_wgu(m, 0)
                    for i in range(1, NI):
                        issue_wgu(m, i)
                    if m < NM - 1:
                        issue_wgu(m + 1, 0)
                    for hc in range(8):
                        for half in range(2):
                            WH = 2 * (GM * 8 + hc) + half
                            if g5 == 0:
                                pc_wait(m * 38 + 22 + 2 * hc + half)
                            if WH >= 4:
                                sy.wait_ge(SM["p_y"], (WH - 4) // 2 + 1)
                            sy.dma_start(out=wd_buf[:, WH % 4, :],
                                         in_=wd_bf[m, :, hc, half * 1408:(half + 1) * 1408]
                                         ).then_inc(SM[f"s_wd{WH % 4}"], 16)
                    if m == 0 and g5 >= 1:
                        pg = g5 - 1
                        sy.wait_ge(SM["v_y"], (pg * NM + NM) * 8)
                        for c in range(8):
                            sy.dma_start(out=y_out[:, c, G5SL[pg]],
                                         in_=x_sb[:, c, G5SL[pg]]).then_inc(SM["s_out"], 16)
            sy.wait_ge(SM["v_y"], (7 * NM + NM) * 8)
            for c in range(8):
                sy.dma_start(out=y_out[:, c, G5SL[7]],
                             in_=x_sb[:, c, G5SL[7]]).then_inc(SM["s_out"], 16)
            sy.wait_ge(SM["s_out"], 16 * 64)

    es.close()
    return nc


_BUILT = {}


def _get_nc():
    if "nc" not in _BUILT:
        _BUILT["nc"] = build_nc()
    return _BUILT["nc"]


def _host_images(L0_attn_x, L0_ffn_gate_w, L0_ffn_base_wgu, L0_ffn_base_wd,
                 L0_ffn_exp_wgu, L0_ffn_exp_wd):
    x = np.asarray(L0_attn_x, dtype=np.float32)
    gate_w = np.asarray(L0_ffn_gate_w, dtype=np.float32)
    base_wgu = np.asarray(L0_ffn_base_wgu, dtype=np.float32)
    base_wd = np.asarray(L0_ffn_base_wd, dtype=np.float32)
    exp_wgu = np.asarray(L0_ffn_exp_wgu, dtype=np.float32)
    exp_wd = np.asarray(L0_ffn_exp_wd, dtype=np.float32)

    x_imgs = []
    for b in range(NCORE):
        xi = np.ascontiguousarray(
            x[b].T.reshape(NCH, 128, S).transpose(1, 0, 2))
        x_imgs.append(xi)

    wgu_img = np.empty((NM, 128, NI, 2048), np.float32)
    wd_img = np.empty((NM, 128, 8, 2816), np.float32)
    for m in range(NM):
        wgu = base_wgu if m == 0 else exp_wgu[m - 1]
        wd = base_wd if m == 0 else exp_wd[m - 1]
        w4 = wgu.reshape(8, 128, 5632)
        gpart = w4[:, :, :2816].reshape(8, 128, NI, 128)
        upart = w4[:, :, 2816:].reshape(8, 128, NI, 128)
        cat = np.concatenate([gpart, upart], axis=3)
        wgu_img[m] = cat.transpose(1, 2, 0, 3).reshape(128, NI, 2048)
        wd4 = wd.reshape(NI, 128, 8, 128)
        wd_img[m] = wd4.transpose(1, 2, 0, 3).reshape(128, 8, 2816)

    gate_img = np.ascontiguousarray(
        gate_w.reshape(8, 128, 8).transpose(1, 0, 2).reshape(128, 64))

    cimg = np.zeros((128, C_W), np.float32)
    cimg[:, C_ID:C_ID + 128] = np.eye(128, dtype=np.float32)
    kk, jj = np.meshgrid(np.arange(128), np.arange(128), indexing="ij")
    cimg[:, C_MASK:C_MASK + 128] = np.where(kk > jj, -1000.0, 0.0)
    cimg[:, C_ONESCOL] = 1.0
    cimg[0, C_ONESROW:C_ONESROW + 128] = 1.0
    cimg[0, C_EPS] = 1e-5
    cimg[0, C_NEGHALF] = -0.5
    cimg[0, C_HALF] = 0.5
    sel = np.zeros((8, 1024), np.float32)
    for e in range(8):
        sel[e, e * 128:(e + 1) * 128] = 1.0
    cimg[0:8, C_SEL:C_SEL + 1024] = sel

    return x_imgs, wgu_img, wd_img, gate_img, cimg


def kernel(L0_attn_x, L0_ffn_gate_w, L0_ffn_base_wgu, L0_ffn_base_wd,
           L0_ffn_exp_wgu, L0_ffn_exp_wd):
    x_imgs, wgu_img, wd_img, gate_img, cimg = _host_images(
        L0_attn_x, L0_ffn_gate_w, L0_ffn_base_wgu, L0_ffn_base_wd,
        L0_ffn_exp_wgu, L0_ffn_exp_wd)
    nc = _get_nc()
    in_maps = [{
        "x_img": x_imgs[b],
        "wgu_img": wgu_img,
        "wd_img": wd_img,
        "gate_img": gate_img,
        "const_img": cimg,
    } for b in range(NCORE)]
    res = run_bass_kernel_spmd(nc, in_maps, core_ids=list(range(NCORE)))
    out = np.empty((NCORE, S, D), np.float32)
    for b in range(NCORE):
        yi = np.asarray(res.results[b]["y_img"], np.float32)
        out[b] = yi.transpose(1, 0, 2).reshape(D, S).T
    return out
